# version 13
# baseline (speedup 1.0000x reference)
import math
import contextlib
import numpy as np
import ml_dtypes
import concourse.bass as bass
import concourse.mybir as mybir
from concourse.bass_utils import run_bass_kernel_spmd

F32 = mybir.dt.float32
BF16 = mybir.dt.bfloat16
U8 = mybir.dt.uint8
AF = mybir.ActivationFunctionType
ALU = mybir.AluOpType
AX = mybir.AxisListType

NCORES = 8
T = 2048
D = 4096
NK = 32
NCH = 16
CW = 256
NTS = 16
NQ = NCH * NTS
RING = 48
RRAW = 6
MTS = 10


def _perm_rows(a):
    a = np.ascontiguousarray(a)
    return np.ascontiguousarray(
        a.reshape(128, 32, a.shape[1]).transpose(1, 0, 2).reshape(4096, a.shape[1]))


_CACHE = {}


def _build_MT():
    if "MT" not in _CACHE:
        n = 4096
        i = np.arange(n)[:, None].astype(np.float64)
        f = np.arange(1, n // 2)[None, :].astype(np.float64)
        ang = 2.0 * np.pi * i * f / n
        M = np.zeros((n, n), dtype=np.float64)
        M[:, 0] = 1.0 / math.sqrt(n)
        M[:, 1:n - 1:2] = math.sqrt(2.0 / n) * np.cos(ang)
        M[:, 2:n - 1:2] = math.sqrt(2.0 / n) * np.sin(ang)
        M[:, n - 1] = ((-1.0) ** np.arange(n)) / math.sqrt(n)
        _CACHE["MT"] = np.ascontiguousarray(M.T.astype(np.float32)).astype(ml_dtypes.bfloat16)
    return _CACHE["MT"]


class _Sem:
    __slots__ = ("h", "n", "name")

    def __init__(self, stack, nc, name):
        self.h = stack.enter_context(nc.semaphore(name))
        self.n = 0
        self.name = name

    def inc(self, instr, k=1):
        instr.then_inc(self.h, k)
        self.n += k
        return instr


def _build_nc(nch=NCH, dbg=False, noai=False):
    key = ("nc", nch, dbg, noai)
    if key in _CACHE:
        return _CACHE[key]
    pmax = (nch - 1) // 2 if nch >= 1 else -1
    nc = bass.Bass("TRN2", target_bir_lowering=False)

    dxT = nc.dram_tensor("xT", [D, T], F32, kind="ExternalInput").ap()
    dxtm = nc.dram_tensor("xtm", [T, D], F32, kind="ExternalInput").ap()
    dwwb = nc.dram_tensor("wwbT", [D, 2 * D], F32, kind="ExternalInput").ap()
    dmkT = nc.dram_tensor("mkT", [D, D], U8, kind="ExternalInput").ap()
    dMT = nc.dram_tensor("MT", [D, D], BF16, kind="ExternalInput").ap()
    dencT = nc.dram_tensor("encT", [D, 48], F32, kind="ExternalInput").ap()
    dcoef0 = nc.dram_tensor("coef0", [3, D, 32], F32, kind="ExternalInput").ap()
    dmsk0 = nc.dram_tensor("msk0", [3, D, 32], U8, kind="ExternalInput").ap()
    dcoef1 = nc.dram_tensor("coef1", [3, D, 8], F32, kind="ExternalInput").ap()
    dmsk1 = nc.dram_tensor("msk1", [3, D, 8], U8, kind="ExternalInput").ap()
    dbA = nc.dram_tensor("biasA", [1, D], F32, kind="ExternalInput").ap()
    dbB = nc.dram_tensor("biasB", [1, D], F32, kind="ExternalInput").ap()
    didn = nc.dram_tensor("idn", [128, 128], F32, kind="ExternalInput").ap()
    dR8 = nc.dram_tensor("R8", [8, CW], F32, kind="ExternalInput").ap()
    dout = nc.dram_tensor("out", [T, D], F32, kind="ExternalOutput").ap()
    if dbg:
        dbg_xh = nc.dram_tensor("dbg_xh", [128, NK * T], BF16, kind="ExternalOutput").ap()
        dbg_e01 = nc.dram_tensor("dbg_e01", [48, T], BF16, kind="ExternalOutput").ap()
        dbg_mr = nc.dram_tensor("dbg_mr", [128, T], BF16, kind="ExternalOutput").ap()
        dbg_inv = nc.dram_tensor("dbg_inv", [128, 16], F32, kind="ExternalOutput").ap()
        dbg_ws = nc.dram_tensor("dbg_ws", [128, NK * 48], BF16, kind="ExternalOutput").ap()
        dbg_wc = nc.dram_tensor("dbg_wc", [128, RING * CW], BF16, kind="ExternalOutput").ap()
        dbg_ah = nc.dram_tensor("dbg_ah", [48, 2 * 512], BF16, kind="ExternalOutput").ap()
        dbg_bx = nc.dram_tensor("dbg_bx", [128, CW], BF16, kind="ExternalOutput").ap()
        dbg_mp = nc.dram_tensor("dbg_mp", [128, 4 * CW], BF16, kind="ExternalOutput").ap()
        dbg_bmi = nc.dram_tensor("dbg_bmi", [128, 8], F32, kind="ExternalOutput").ap()

    with contextlib.ExitStack() as st:
        E = st.enter_context
        sXH = E(nc.sbuf_tensor("sXH", [128, NK * T], BF16))
        sWC = E(nc.sbuf_tensor("sWC", [128, RING * CW], BF16))
        sRWB = E(nc.sbuf_tensor("sRWB", [128, RRAW * 2 * CW], F32))
        sMP = E(nc.sbuf_tensor("sMP", [128, 4 * CW], BF16))
        sBS = E(nc.sbuf_tensor("sBS", [128, CW], F32))
        sBMI = E(nc.sbuf_tensor("sBMI", [128, 8], F32))
        sBM2 = E(nc.sbuf_tensor("sBM2", [128, 8], F32))
        sSPC = E(nc.sbuf_tensor("sSPC", [128, 2], F32))
        sBIT = E(nc.sbuf_tensor("sBIT", [16, 128], BF16))
        sBX = E(nc.sbuf_tensor("sBX", [128, CW], BF16))
        sMTb = E(nc.sbuf_tensor("sMTb", [128, MTS * 512], BF16))
        sAH = E(nc.sbuf_tensor("sAH", [48, 2 * 512], BF16))
        sE01 = E(nc.sbuf_tensor("sE01", [48, T], BF16))
        sENC = E(nc.sbuf_tensor("sENC", [128, NK * 48], BF16))
        sWS = E(nc.sbuf_tensor("sWS", [128, NK * 48], BF16))
        sXS = E(nc.sbuf_tensor("sXS", [128, 2 * 512], F32))
        sSQ = E(nc.sbuf_tensor("sSQ", [128, 512], BF16))
        sXR = E(nc.sbuf_tensor("sXR", [128, 3 * CW], F32))
        sOS = E(nc.sbuf_tensor("sOS", [128, 2 * CW], F32))
        sSCR = E(nc.sbuf_tensor("sSCR", [1, 512], F32))
        sSCR2 = E(nc.sbuf_tensor("sSCR2", [1, 512], F32))
        sMHI = E(nc.sbuf_tensor("sMHI", [1, 512], BF16))
        sMLO = E(nc.sbuf_tensor("sMLO", [1, 512], BF16))
        sIDN = E(nc.sbuf_tensor("sIDN", [128, 128], F32))
        sR8 = E(nc.sbuf_tensor("sR8", [8, CW], BF16))
        sINVC = E(nc.sbuf_tensor("sINVC", [128, 16], F32))
        sINVD = E(nc.sbuf_tensor("sINVD", [128, 16], F32))
        sONC = E(nc.sbuf_tensor("sONC", [128, 1], BF16))
        sONR = E(nc.sbuf_tensor("sONR", [1, 128], BF16))

        pm = [E(nc.psum_tensor(f"pm{i}", [128, 512], F32)) for i in range(4)]
        pst = E(nc.psum_tensor("pst", [128, 512], F32))
        paux = E(nc.psum_tensor("paux", [128, 512], F32))
        pbm = E(nc.psum_tensor("pbm", [128, 512], F32))
        pA = E(nc.psum_tensor("pA", [128, 512], F32))
        BK = [pst, paux, pbm, pA]

        S = lambda name: _Sem(st, nc, name)
        sLx, sLk, sLs, sW, sMT = S("sLx"), S("sLk"), S("sLs"), S("sW"), S("sMT")
        sXre, sSt, sE40 = S("sXre"), S("sSt"), S("sE40")
        sMk = [S(f"sMk{i}") for i in range(4)]
        sXc, sSq, sTsq = S("sXc"), S("sSq"), S("sTsq")
        sVr, sSd2, sTiv, sIvc = S("sVr"), S("sSd2"), S("sTiv"), S("sIvc")
        sMh, sMb, sFin, sXh, sTer, sE1 = S("sMh"), S("sMb"), S("sFin"), S("sXh"), S("sTer"), S("sE1")
        sWs, sWp = S("sWs"), S("sWp")
        sWcR, sWcF, sPs, sEpi = S("sWcR"), S("sWcF"), S("sPs"), S("sEpi")
        sAmm, sAh, sBd = S("sAmm"), S("sAh"), S("sBd")
        sTr, sBt, sBit, sBx, sBm = S("sTr"), S("sBt"), S("sBit"), S("sBx"), S("sBm")
        sRd = S("sRd")

        blk = E(nc.Block())
        lkm = {}

        @blk.sync
        def _(sy):
            sLs.inc(sy.dma_start(sIDN[:, :], didn[:, :]), 16)
            for i in range(128):
                j, tg = i // 4, i % 4
                if i >= 2:
                    sy.wait_ge(sXc.h, i - 1)
                sLx.inc(sy.dma_start(
                    sXS[:, (i % 2) * 512:(i % 2) * 512 + 512],
                    dxT[j * 128:(j + 1) * 128, tg * 512:(tg + 1) * 512]), 16)
            for g in range(nch * NK):
                c, k = g // NK, g % NK
                if g >= RRAW:
                    sy.wait_ge(sWcR.h, g - RRAW + 1)
                sl = (g % RRAW) * 2 * CW
                sW.inc(sy.dma_start(
                    sRWB[:, sl:sl + 2 * CW],
                    dwwb[k * 128:(k + 1) * 128, c * 2 * CW:(c + 1) * 2 * CW]), 16)

        @blk.gpsimd
        def _(gp):
            ntile_max = 32 * (pmax + 1)

            def mt_issue(gp_, nt):
                sMT.inc(gp_.dma_start(
                    sMTb[:, (nt % MTS) * 512:(nt % MTS) * 512 + 512],
                    dMT[(nt % 32) * 128:(nt % 32 + 1) * 128,
                        (nt // 32) * 512:(nt // 32 + 1) * 512]), 16)

            npiece = 0
            for e in range(3):
                for h in range(2):
                    if npiece < min(MTS, ntile_max):
                        mt_issue(gp, npiece)
                    if npiece > 0:
                        gp.wait_ge(sWp.h, npiece)
                    src = dcoef0[e].rearrange("(j p) r -> p j r", p=128)[:, h * 16:(h + 1) * 16, :]
                    sLk.inc(gp.dma_start(
                        sXR[:, 0:512].rearrange("p (j r) -> p j r", r=32), src), 16)
                    srcm = dmsk0[e].rearrange("(j p) r -> p j r", p=128)[:, h * 16:(h + 1) * 16, :]
                    sLk.inc(gp.dma_start(
                        sOS[:, 0:512].rearrange("p (j r) -> p j r", r=32), srcm), 16)
                    lkm[("c0", e, h)] = sLk.n
                    npiece += 1
            for e in range(3):
                if npiece < min(MTS, ntile_max):
                    mt_issue(gp, npiece)
                gp.wait_ge(sWp.h, npiece)
                src = dcoef1[e].rearrange("(j p) r -> p j r", p=128)
                sLk.inc(gp.dma_start(
                    sXR[:, 0:256].rearrange("p (j r) -> p j r", r=8), src), 16)
                srcm = dmsk1[e].rearrange("(j p) r -> p j r", p=128)
                sLk.inc(gp.dma_start(
                    sOS[:, 0:256].rearrange("p (j r) -> p j r", r=8), srcm), 16)
                lkm[("c1", e)] = sLk.n
                npiece += 1
            for nt in range(9, min(MTS, ntile_max)):
                mt_issue(gp, nt)
            src = dencT.rearrange("(j p) r -> p j r", p=128)
            sLk.inc(gp.dma_start(sENC.rearrange("p (j r) -> p j r", r=48), src), 16)
            sLk.inc(gp.dma_start(sR8[:, :], dR8[:, :]), 16)
            lkm["consts"] = sLk.n

            def bias_dmas(p):
                if p >= 2:
                    gp.wait_ge(sPs.h, 16 * (2 * p - 2))
                hb = (p % 2) * 512
                sBd.inc(gp.dma_start(sAH[40:41, hb:hb + 512],
                                     dbA[0:1, p * 512:(p + 1) * 512]), 16)
                sBd.inc(gp.dma_start(sAH[41:42, hb:hb + 512],
                                     dbB[0:1, p * 512:(p + 1) * 512]), 16)

            def chains(g):
                for l in range(8):
                    for i in range(4):
                        jj = i * 8 + l
                        if l == 0:
                            if g == 0:
                                gp.wait_ge(sWs.h, 1)
                            gp.wait_ge(sRd.h, g)
                        else:
                            gp.wait_ge(sMk[i].h, 16 * (8 * g + l))
                        dst = sMP[:, i * CW:(i + 1) * CW]
                        src = dmkT[jj * 128:(jj + 1) * 128, g * CW:(g + 1) * CW]
                        if l == 0:
                            sMk[i].inc(gp.dma_start(dst, src), 16)
                        else:
                            sMk[i].inc(gp.dma_start(dst, src, accum_op=ALU.add), 16)

            if nch >= 1:
                chains(0)
            if nch >= 2:
                chains(1)
            for tg in range(4):
                gp.wait_ge(sSd2.h, tg + 1)
                for r in (40, 41):
                    sE40.inc(gp.dma_start(sE01[r:r + 1, tg * 512:(tg + 1) * 512],
                                          sSCR[0:1, :]), 16)
            for nt in range(min(MTS, ntile_max), min(41, ntile_max)):
                gp.wait_ge(sAmm.h, nt - (MTS - 1))
                mt_issue(gp, nt)
            if pmax >= 0:
                bias_dmas(0)
            for c in range(nch):
                if c + 2 < nch:
                    chains(c + 2)
                if c % 2 == 1 and (c + 1) // 2 <= pmax:
                    bias_dmas((c + 1) // 2)

        @blk.scalar
        def _(sc):
            for i in range(128):
                j, tg = i // 4, i % 4
                sc.wait_ge(sLx.h, 16 * min(i + 2, 128))
                base = j * T + tg * 512
                sXc.inc(sc.copy(sXH[:, base:base + 512],
                                sXS[:, (i % 2) * 512:(i % 2) * 512 + 512]))
                if i > 0:
                    sc.wait_ge(sTsq.h, i)
                sSq.inc(sc.square(sSQ[:, :], sXH[:, base:base + 512]))
            for tg in range(4):
                sc.wait_ge(sVr.h, tg + 1)
                sSd2.inc(sc.activation(sSCR[0:1, :], sSCR2[0:1, :], AF.Sqrt))
                for sub in range(4):
                    sc.wait_ge(sTiv.h, 4 * tg + sub + 1)
                    sIvc.inc(sc.copy(sINVC[:, 4 * tg + sub:4 * tg + sub + 1],
                                     BK[tg][:, sub:sub + 1]))
            def sign_op(g):
                sc.wait_ge(sRd.h, g + 1)
                if g > 0:
                    sc.wait_ge(sBt.h, g)
                sTr.inc(sc.activation(sBM2[:, :], sBMI[:, :], AF.Sign))
            if nch >= 1:
                sign_op(0)
            for Q in range(-2, 16 * nch):
                if Q >= 0 and Q % NTS == 0 and Q // NTS + 1 < nch:
                    sign_op(Q // NTS + 1)
                Qx = Q + 2
                if Qx < 16 * nch:
                    cx, qx = Qx // NTS, Qx % NTS
                    if Qx < 2:
                        sc.wait_ge(sWp.h, 9)
                    if Qx >= 3:
                        sc.wait_ge(sEpi.h, Qx - 2)
                    sXre.inc(sc.dma_start(sXR[:, (Qx % 3) * CW:(Qx % 3) * CW + CW],
                                          dxtm[qx * 128:(qx + 1) * 128, cx * CW:(cx + 1) * CW]), 16)
                if Q >= 0:
                    c, q = Q // NTS, Q % NTS
                    sc.wait_ge(sEpi.h, Q + 1)
                    sSt.inc(sc.dma_start(dout[q * 128:(q + 1) * 128, c * CW:(c + 1) * CW],
                                         sOS[:, (Q % 2) * CW:(Q % 2) * CW + CW]), 16)
                    if 41 <= Q + 41 < 32 * (pmax + 1):
                        nt = Q + 41
                        sMT.inc(sc.dma_start(
                            sMTb[:, (nt % MTS) * 512:(nt % MTS) * 512 + 512],
                            dMT[(nt % 32) * 128:(nt % 32 + 1) * 128,
                                (nt // 32) * 512:(nt // 32 + 1) * 512]), 16)
            sc.wait_ge(sSt.h, 16 * 16 * nch)
            if dbg:
                sc.wait_ge(sXh.h, 32)
                sc.dma_start(dbg_xh[:, :], sXH[:, :]).then_inc(sSt.h, 16)
                sc.wait_ge(sE1.h, 4)
                sc.wait_ge(sE40.h, 128)
                sc.dma_start(dbg_e01[:, :], sE01[:, :]).then_inc(sSt.h, 16)
                sc.wait_ge(sFin.h, 4)
                sc.dma_start(dbg_mr[:, :], sWC[:, 0:T]).then_inc(sSt.h, 16)
                sc.dma_start(dbg_inv[:, :], sINVD[:, :]).then_inc(sSt.h, 16)
                sc.wait_ge(sWs.h, 1)
                sc.dma_start(dbg_ws[:, :], sWS[:, :]).then_inc(sSt.h, 16)
                if nch >= 1:
                    sc.wait_ge(sWcR.h, min(nch * NK, RING))
                    sc.dma_start(dbg_wc[:, :], sWC[:, :]).then_inc(sSt.h, 16)
                    sc.wait_ge(sAh.h, 1)
                    sc.wait_ge(sBd.h, 32)
                    sc.dma_start(dbg_ah[:, :], sAH[:, :]).then_inc(sSt.h, 16)
                    sc.wait_ge(sBm.h, 1)
                    sc.dma_start(dbg_bx[:, :], sBX[:, :]).then_inc(sSt.h, 16)
                    sc.wait_ge(sTr.h, 1)
                    sc.dma_start(dbg_mp[:, :], sMP[:, :]).then_inc(sSt.h, 16)
                    sc.dma_start(dbg_bmi[:, :], sBM2[:, :]).then_inc(sSt.h, 16)
                sc.wait_ge(sSt.h, 16 * 16 * nch + 16 * (10 if nch >= 1 else 5))

        @blk.vector
        def _(v):
            v.memset(sWS[:, :], 0.0)
            v.memset(sE01[32:48, :], 0.0)
            v.memset(sAH[32:48, :], 0.0)
            v.memset(sONC[:, :], 1.0)
            v.memset(sONR[:, :], 1.0)
            wtmp = sWC[:, 46 * CW:46 * CW + 512]
            npiece = 0
            for e in range(3):
                for h in range(2):
                    v.wait_ge(sLk.h, lkm[("c0", e, h)])
                    acc = sMP[:, h * 512:h * 512 + 512]
                    if e == 0:
                        ins = v.tensor_tensor(acc, sXR[:, 0:512], sOS[:, 0:512], ALU.mult)
                    else:
                        v.tensor_tensor(wtmp, sXR[:, 0:512], sOS[:, 0:512], ALU.mult)
                        ins = v.tensor_tensor(acc, acc, wtmp, ALU.add)
                    npiece += 1
                    sWp.inc(ins)
            for e in range(3):
                v.wait_ge(sLk.h, lkm[("c1", e)])
                if e == 0:
                    ins = v.tensor_tensor(sBS[:, 0:256], sXR[:, 0:256], sOS[:, 0:256], ALU.mult)
                else:
                    v.tensor_tensor(wtmp[:, 0:256], sXR[:, 0:256], sOS[:, 0:256], ALU.mult)
                    ins = v.tensor_tensor(sBS[:, 0:256], sBS[:, 0:256], wtmp[:, 0:256], ALU.add)
                npiece += 1
                sWp.inc(ins)
            for j in range(NK):
                jj = j % 16
                v.tensor_scalar(sWS[:, j * 48:j * 48 + 32],
                                sMP[:, (j // 16) * 512 + jj * 32:(j // 16) * 512 + jj * 32 + 32],
                                0.5, None, ALU.mult)
                ins = v.tensor_scalar(sWS[:, j * 48 + 32:j * 48 + 40],
                                      sBS[:, j * 8:j * 8 + 8], 0.5, None, ALU.mult)
                if j == NK - 1:
                    sWs.inc(ins)
            def front(g):
                for i in range(4):
                    v.wait_ge(sMk[i].h, 16 * 8 * (g + 1))
                v.tensor_tensor(sBS[:, :], sMP[:, 0:CW], sMP[:, CW:2 * CW], ALU.add)
                v.tensor_tensor(sMP[:, 2 * CW:3 * CW], sMP[:, 2 * CW:3 * CW],
                                sMP[:, 3 * CW:4 * CW], ALU.add)
                v.tensor_tensor(sBS[:, :], sBS[:, :], sMP[:, 2 * CW:3 * CW], ALU.add)
                if g > 0:
                    v.wait_ge(sTr.h, g)
                v.memset(sSPC[:, :], 0.0)
                sRd.inc(v.tensor_reduce(sBMI[:, :], sBS.rearrange("p (a b) -> p a b", b=32),
                                        AX.X, ALU.add))

            if nch >= 1:
                front(0)
            v.wait_ge(sTsq.h, 128)
            for tg in range(4):
                rm = pm[tg][0:1, 0:512]
                rq = BK[tg][0:1, 0:512]
                v.tensor_scalar(sSCR[0:1, :], rm, 1.0 / 4096.0, None, ALU.mult)
                v.tensor_scalar(sSCR2[0:1, :], rq, 1.0 / 4096.0, None, ALU.mult)
                v.tensor_tensor(rm, sSCR[0:1, :], sSCR[0:1, :], ALU.mult)
                v.tensor_copy(sMHI[0:1, :], sSCR[0:1, :])
                v.tensor_tensor(sSCR2[0:1, :], sSCR2[0:1, :], rm, ALU.subtract)
                v.tensor_tensor(rm, sSCR[0:1, :], sMHI[0:1, :], ALU.subtract)
                sVr.inc(v.tensor_scalar(sSCR2[0:1, :], sSCR2[0:1, :], 1e-5, None, ALU.add))
                sMh.inc(v.tensor_copy(sMLO[0:1, :], rm))
                v.wait_ge(sTiv.h, 4 * (tg + 1))
                v.wait_ge(sE40.h, 32 * (tg + 1))
                v.wait_ge(sMb.h, tg + 1)
                sFin.inc(v.tensor_copy(sWC[:, tg * 512:(tg + 1) * 512], BK[tg][:, 0:512]))
            v.reciprocal(sINVD[:, :], sINVC[:, :])
            for j in range(NK):
                v.wait_ge(sTsq.h, 4 * (j + 1))
                v.wait_ge(sFin.h, 4)
                sXh.inc(v.tensor_tensor(sXH[:, j * T:(j + 1) * T],
                                        sXH[:, j * T:(j + 1) * T], sWC[:, 0:T], ALU.subtract))
            for tg in range(4):
                v.wait_ge(sTer.h, tg + 1)
                sE1.inc(v.tensor_copy(sE01[0:40, tg * 512:(tg + 1) * 512],
                                      pm[tg][0:40, 0:512]))

            def bm_copies(g):
                v.wait_ge(sBt.h, g + 1)
                sBit.inc(v.tensor_copy(sBIT[0:8, 0:128], paux[0:8, 0:128]))
                v.wait_ge(sBx.h, g + 1)
                sBm.inc(v.tensor_copy(sBX[:, :], pbm[:, 0:CW]))

            def ahat_evac(p):
                v.wait_ge(sAmm.h, 32 * (p + 1))
                if p >= 2:
                    v.wait_ge(sPs.h, 16 * (2 * p - 2))
                hb = (p % 2) * 512
                sAh.inc(v.tensor_copy(sAH[0:40, hb:hb + 512], pA[0:40, 0:512]))

            def epi(Q):
                q = Q % NTS
                v.wait_ge(sPs.h, Q + 1)
                v.wait_ge(sXre.h, 16 * min(Q + 2, 16 * nch))
                if Q >= 1:
                    v.wait_ge(sSt.h, 16 * Q)
                sEpi.inc(v.scalar_tensor_tensor(
                    sOS[:, (Q % 2) * CW:(Q % 2) * CW + CW],
                    pm[Q % 4][:, 0:CW], sINVD[:, q:q + 1],
                    sXR[:, (Q % 3) * CW:(Q % 3) * CW + CW], ALU.mult, ALU.add))

            def combines(c, k0=0, k1=NK):
                for k in range(k0, k1):
                    g = c * NK + k
                    v.wait_ge(sW.h, min(16 * (g + 4), 16 * nch * NK))
                    if g >= RING:
                        v.wait_ge(sWcF.h, g + 1 - RING)
                    sl = (g % RRAW) * 2 * CW
                    v.tensor_tensor(pst[:, 0:CW], sRWB[:, sl:sl + CW], sBX[:, :], ALU.mult)
                    sWcR.inc(v.tensor_tensor(
                        sWC[:, (g % RING) * CW:(g % RING) * CW + CW],
                        pst[:, 0:CW], sRWB[:, sl + CW:sl + 2 * CW], ALU.add))

            if nch >= 1:
                bm_copies(0)
                v.wait_ge(sLk.h, lkm["consts"])
                ahat_evac(0)
                if noai:
                    for pp in range(1, pmax + 1):
                        ahat_evac(pp)
                combines(0)
            for c in range(nch):
                if c + 1 < nch:
                    front(c + 1)
                for q in range(0, 4):
                    epi(16 * c + q)
                if c + 1 < nch:
                    bm_copies(c + 1)
                for q in range(4, 13):
                    epi(16 * c + q)
                if c % 2 == 1 and (c + 1) // 2 <= pmax and not noai:
                    ahat_evac((c + 1) // 2)
                if c + 1 < nch:
                    combines(c + 1, 0, 2)
                for q in range(13, 16):
                    epi(16 * c + q)
                if c + 1 < nch:
                    combines(c + 1, 2, NK)

        @blk.tensor
        def _(t):
            ntile_max = 32 * (pmax + 1)
            for i in range(128):
                j, tg = i // 4, i % 4
                base = j * T + tg * 512
                t.wait_ge(sXc.h, i + 1)
                t.matmul(pm[tg][0:1, 0:512], sONC[:, :], sXH[:, base:base + 512],
                         start=(j == 0), stop=(j == 31))
                t.wait_ge(sSq.h, i + 1)
                sTsq.inc(t.matmul(BK[tg][0:1, 0:512], sONC[:, :], sSQ[:, :],
                                  start=(j == 0), stop=(j == 31)))
            t.wait_ge(sLs.h, 16)
            for tg in range(4):
                t.wait_ge(sSd2.h, tg + 1)
                for sub in range(4):
                    sTiv.inc(t.matmul(BK[tg][:, sub:sub + 1],
                                      sSCR[0:1, sub * 128:sub * 128 + 128],
                                      sIDN[0:1, 0:1], start=True, stop=True))
                t.wait_ge(sIvc.h, 4 * (tg + 1))
                t.wait_ge(sMh.h, tg + 1)
                t.matmul(BK[tg][:, 0:512], sONR[:, :], sMHI[0:1, :],
                         start=True, stop=False)
                sMb.inc(t.matmul(BK[tg][:, 0:512], sONR[:, :], sMLO[0:1, :],
                                 start=False, stop=True))
            t.wait_ge(sFin.h, 4)
            t.wait_ge(sLk.h, lkm["consts"])
            if pmax >= 0:
                t.wait_ge(sWs.h, 1)
            for j in range(NK):
                t.wait_ge(sXh.h, j + 1)
                for tg in range(4):
                    mm = t.matmul(pm[tg][0:48, 0:512], sENC[:, j * 48:(j + 1) * 48],
                                  sXH[:, j * T + tg * 512:j * T + tg * 512 + 512],
                                  start=(j == 0), stop=(j == 31))
                    if j == 31:
                        sTer.inc(mm)
                if pmax >= 0:
                    t.wait_ge(sMT.h, 16 * min(j + 5, ntile_max))
                    sAmm.inc(t.matmul(pA[0:48, 0:512], sWS[:, j * 48:(j + 1) * 48],
                                      sMTb[:, (j % MTS) * 512:(j % MTS) * 512 + 512],
                                      start=(j == 0), stop=(j == 31)))
            def a_mms(pp):
                t.wait_ge(sAh.h, pp)
                for o in range(32):
                    gidx = 32 * pp + o
                    t.wait_ge(sMT.h, 16 * min(gidx + 5, ntile_max))
                    mm = t.matmul(pA[0:48, 0:512], sWS[:, o * 48:(o + 1) * 48],
                                  sMTb[:, (gidx % MTS) * 512:(gidx % MTS) * 512 + 512],
                                  start=(o == 0), stop=(o == 31))
                    sAmm.inc(mm)
            if nch >= 1:
                t.wait_ge(sTr.h, 1)
                sBt.inc(t.transpose(paux[0:8, 0:128], sBM2[:, :], sIDN[:, :]))
                t.wait_ge(sBit.h, 1)
                sBx.inc(t.matmul(pbm[:, 0:CW], sBIT[0:8, 0:128], sR8[0:8, 0:CW],
                                 start=True, stop=True))
            if noai:
                for pp in range(1, pmax + 1):
                    a_mms(pp)
            for c in range(nch):
                p2 = c // 2 + 1
                for ts in range(NTS):
                    Q = 16 * c + ts
                    if Q >= 4:
                        t.wait_ge(sEpi.h, Q - 3)
                    if ts == 0:
                        t.wait_ge(sAh.h, c // 2 + 1)
                        t.wait_ge(sBd.h, 32 * (c // 2 + 1))
                        if c == 0:
                            t.wait_ge(sE1.h, 4)
                            t.wait_ge(sE40.h, 128)
                    for k in range(NK):
                        g = c * NK + k
                        if ts == 0:
                            t.wait_ge(sWcR.h, g + 1)
                        mm = t.matmul(pm[Q % 4][:, 0:CW],
                                      sXH[:, k * T + ts * 128:k * T + ts * 128 + 128],
                                      sWC[:, (g % RING) * CW:(g % RING) * CW + CW],
                                      start=(k == 0), stop=False)
                        if ts == NTS - 1:
                            sWcF.inc(mm)
                    hb = ((c // 2) % 2) * 512 + (c % 2) * CW
                    sPs.inc(t.matmul(pm[Q % 4][:, 0:CW],
                                     sE01[0:48, ts * 128:(ts + 1) * 128],
                                     sAH[0:48, hb:hb + CW],
                                     start=False, stop=True))
                    if ts == 4 and c + 1 < nch:
                        t.wait_ge(sTr.h, c + 2)
                        sBt.inc(t.transpose(paux[0:8, 0:128], sBM2[:, :], sIDN[:, :]))
                    if ts == 5 and c + 1 < nch:
                        t.wait_ge(sBit.h, c + 2)
                        t.wait_ge(sBm.h, c + 1)
                        sBx.inc(t.matmul(pbm[:, 0:CW], sBIT[0:8, 0:128],
                                         sR8[0:8, 0:CW], start=True, stop=True))
                    if p2 <= pmax and not noai:
                        o = (c % 2) * 16 + ts
                        gidx = 32 * p2 + o
                        t.wait_ge(sMT.h, 16 * min(gidx + 5, ntile_max))
                        if o == 0:
                            t.wait_ge(sAh.h, p2)
                        mm = t.matmul(pA[0:48, 0:512], sWS[:, o * 48:(o + 1) * 48],
                                      sMTb[:, (gidx % MTS) * 512:(gidx % MTS) * 512 + 512],
                                      start=(o == 0), stop=(o == 31))
                        sAmm.inc(mm)

        if nch != NCH:
            _CACHE[key] = nc
            return nc
        assert sLx.n == 16 * 128 and sW.n == 16 * NCH * NK
        assert sMT.n == 16 * 256 and sXre.n == 16 * NQ and sSt.n == 16 * NQ
        assert all(sm.n == 16 * 8 * NCH for sm in sMk)
        assert sXc.n == 128 and sSq.n == 128 and sTsq.n == 128
        assert sVr.n == 4 and sSd2.n == 4 and sTiv.n == 16 and sIvc.n == 16
        assert sMh.n == 4 and sMb.n == 4 and sFin.n == 4 and sE40.n == 16 * 8
        assert sXh.n == NK and sTer.n == 4 and sE1.n == 4 and sWs.n == 1 and sWp.n == 9
        assert sWcR.n == NCH * NK and sWcF.n == NCH * NK and sPs.n == NQ and sEpi.n == NQ
        assert sAmm.n == 256 and sAh.n == 8 and sBd.n == 32 * 8
        assert sTr.n == NCH and sBt.n == NCH and sBit.n == NCH and sBx.n == NCH and sBm.n == NCH
        assert sRd.n == NCH

        _CACHE[key] = nc
    return nc


def kernel(L0_attn_data, L0_attn_mask, L0_attn_weight, L0_attn_bias,
           L0_ffn_W_base, L0_ffn_b_base, L0_ffn_enc0, L0_ffn_enc1,
           L0_ffn_coefs0, L0_ffn_coefs1, L0_ffn_mask0, L0_ffn_mask1):
    asc = np.ascontiguousarray
    x = asc(np.asarray(L0_attn_data, dtype=np.float32)).reshape(NCORES * T, D)

    _wt = _perm_rows(asc(np.asarray(L0_attn_weight, np.float32)).T).reshape(D, NCH, CW)
    _wb = _perm_rows(asc(np.asarray(L0_ffn_W_base, np.float32)).T).reshape(D, NCH, CW)
    wwbT = asc(np.concatenate([_wt, _wb], axis=2).reshape(D, 2 * D))
    m32 = asc(np.asarray(L0_attn_mask, np.int32).T)
    mkT = _perm_rows(asc(m32.view(np.uint8)[:, 0::4]))
    MT = _build_MT()
    encT = np.zeros((D, 48), dtype=np.float32)
    encT[:, 0:32] = np.asarray(L0_ffn_enc0, np.float32).T
    encT[:, 32:40] = np.asarray(L0_ffn_enc1, np.float32).T
    encT = _perm_rows(encT)
    coef0 = asc(np.asarray(L0_ffn_coefs0, np.float32))
    coef1 = asc(np.asarray(L0_ffn_coefs1, np.float32))
    msk0 = asc(np.asarray(L0_ffn_mask0)).view(np.uint8)
    msk1 = asc(np.asarray(L0_ffn_mask1)).view(np.uint8)
    biasA = asc(np.asarray(L0_attn_bias, np.float32)).reshape(1, D)
    biasB = asc(np.asarray(L0_ffn_b_base, np.float32)).reshape(1, D)
    idn = np.eye(128, dtype=np.float32)
    R8 = np.kron(np.eye(8, dtype=np.float32), np.ones((1, 32), dtype=np.float32))
    R8 = asc(R8)

    shared = dict(wwbT=wwbT, mkT=mkT, MT=MT, encT=encT,
                  coef0=coef0, msk0=msk0, coef1=coef1, msk1=msk1,
                  biasA=biasA, biasB=biasB, idn=idn, R8=R8)
    in_maps = []
    for ci in range(NCORES):
        xs = x[ci * T:(ci + 1) * T]
        m = dict(shared)
        m["xT"] = _perm_rows(asc(xs.T))
        m["xtm"] = asc(xs)
        in_maps.append(m)

    nc = _build_nc()
    res = run_bass_kernel_spmd(nc, in_maps, core_ids=list(range(NCORES)))
    out = np.stack([np.asarray(res.results[i]["out"], dtype=np.float32)
                    for i in range(NCORES)], axis=0)
    return out.reshape(NCORES, T, D)


# version 14
# speedup vs baseline: 1.0679x; 1.0679x over previous
import math
import contextlib
import numpy as np
import ml_dtypes
import concourse.bass as bass
import concourse.mybir as mybir
from concourse.bass_utils import run_bass_kernel_spmd

F32 = mybir.dt.float32
BF16 = mybir.dt.bfloat16
U8 = mybir.dt.uint8
AF = mybir.ActivationFunctionType
ALU = mybir.AluOpType
AX = mybir.AxisListType

NCORES = 8
T = 2048
D = 4096
NK = 32
NCH = 16
CW = 256
NTS = 16
NQ = NCH * NTS
RING = 56
RRAW = 5
MTS = 8


def _perm_rows(a):
    a = np.ascontiguousarray(a)
    return np.ascontiguousarray(
        a.reshape(128, 32, a.shape[1]).transpose(1, 0, 2).reshape(4096, a.shape[1]))


_CACHE = {}


def _build_MT():
    if "MT" not in _CACHE:
        n = 4096
        i = np.arange(n)[:, None].astype(np.float64)
        f = np.arange(1, n // 2)[None, :].astype(np.float64)
        ang = 2.0 * np.pi * i * f / n
        M = np.zeros((n, n), dtype=np.float64)
        M[:, 0] = 1.0 / math.sqrt(n)
        M[:, 1:n - 1:2] = math.sqrt(2.0 / n) * np.cos(ang)
        M[:, 2:n - 1:2] = math.sqrt(2.0 / n) * np.sin(ang)
        M[:, n - 1] = ((-1.0) ** np.arange(n)) / math.sqrt(n)
        _CACHE["MT"] = np.ascontiguousarray(M.T.astype(np.float32)).astype(ml_dtypes.bfloat16)
    return _CACHE["MT"]


class _Sem:
    __slots__ = ("h", "n", "name")

    def __init__(self, stack, nc, name):
        self.h = stack.enter_context(nc.semaphore(name))
        self.n = 0
        self.name = name

    def inc(self, instr, k=1):
        instr.then_inc(self.h, k)
        self.n += k
        return instr


def _build_nc(nch=NCH, dbg=False, noai=False):
    key = ("nc", nch, dbg, noai)
    if key in _CACHE:
        return _CACHE[key]
    pmax = (nch - 1) // 2 if nch >= 1 else -1
    nc = bass.Bass("TRN2", target_bir_lowering=False)

    dxT = nc.dram_tensor("xT", [D, T], F32, kind="ExternalInput").ap()
    dxtm = nc.dram_tensor("xtm", [T, D], F32, kind="ExternalInput").ap()
    dwwb = nc.dram_tensor("wwbT", [D, 2 * D], F32, kind="ExternalInput").ap()
    dmkT = nc.dram_tensor("mkT", [D, D], U8, kind="ExternalInput").ap()
    dMT = nc.dram_tensor("MT", [D, D], BF16, kind="ExternalInput").ap()
    dencT = nc.dram_tensor("encT", [D, 48], F32, kind="ExternalInput").ap()
    dcoef0 = nc.dram_tensor("coef0", [3, D, 32], F32, kind="ExternalInput").ap()
    dmsk0 = nc.dram_tensor("msk0", [3, D, 32], U8, kind="ExternalInput").ap()
    dcoef1 = nc.dram_tensor("coef1", [3, D, 8], F32, kind="ExternalInput").ap()
    dmsk1 = nc.dram_tensor("msk1", [3, D, 8], U8, kind="ExternalInput").ap()
    dbA = nc.dram_tensor("biasA", [1, D], F32, kind="ExternalInput").ap()
    dbB = nc.dram_tensor("biasB", [1, D], F32, kind="ExternalInput").ap()
    didn = nc.dram_tensor("idn", [128, 128], F32, kind="ExternalInput").ap()
    dR8 = nc.dram_tensor("R8", [8, CW], F32, kind="ExternalInput").ap()
    dout = nc.dram_tensor("out", [T, D], F32, kind="ExternalOutput").ap()
    if dbg:
        dbg_xh = nc.dram_tensor("dbg_xh", [128, NK * T], BF16, kind="ExternalOutput").ap()
        dbg_e01 = nc.dram_tensor("dbg_e01", [48, T], BF16, kind="ExternalOutput").ap()
        dbg_mr = nc.dram_tensor("dbg_mr", [128, T], BF16, kind="ExternalOutput").ap()
        dbg_inv = nc.dram_tensor("dbg_inv", [128, 16], F32, kind="ExternalOutput").ap()
        dbg_ws = nc.dram_tensor("dbg_ws", [128, NK * 48], BF16, kind="ExternalOutput").ap()
        dbg_wc = nc.dram_tensor("dbg_wc", [128, RING * CW], BF16, kind="ExternalOutput").ap()
        dbg_ah = nc.dram_tensor("dbg_ah", [48, 2 * 512], BF16, kind="ExternalOutput").ap()
        dbg_bx = nc.dram_tensor("dbg_bx", [128, CW], BF16, kind="ExternalOutput").ap()
        dbg_mp = nc.dram_tensor("dbg_mp", [128, 4 * CW], BF16, kind="ExternalOutput").ap()
        dbg_bmi = nc.dram_tensor("dbg_bmi", [128, 8], F32, kind="ExternalOutput").ap()

    with contextlib.ExitStack() as st:
        E = st.enter_context
        sXH = E(nc.sbuf_tensor("sXH", [128, NK * T], BF16))
        sWC = E(nc.sbuf_tensor("sWC", [128, RING * CW], BF16))
        sRWB = E(nc.sbuf_tensor("sRWB", [128, RRAW * 2 * CW], F32))
        sMP = E(nc.sbuf_tensor("sMP", [128, 4 * CW], BF16))
        sBS = E(nc.sbuf_tensor("sBS", [128, CW], F32))
        sBMI = E(nc.sbuf_tensor("sBMI", [128, 8], F32))
        sBM2 = E(nc.sbuf_tensor("sBM2", [128, 8], F32))
        sSPC = E(nc.sbuf_tensor("sSPC", [128, 2], F32))
        sBIT = E(nc.sbuf_tensor("sBIT", [16, 128], BF16))
        sBX = E(nc.sbuf_tensor("sBX", [128, CW], BF16))
        sMTb = E(nc.sbuf_tensor("sMTb", [128, MTS * 512], BF16))
        sAH = E(nc.sbuf_tensor("sAH", [48, 2 * 512], BF16))
        sE01 = E(nc.sbuf_tensor("sE01", [48, T], BF16))
        sENC = E(nc.sbuf_tensor("sENC", [128, NK * 48], BF16))
        sWS = E(nc.sbuf_tensor("sWS", [128, NK * 48], BF16))
        sXS = E(nc.sbuf_tensor("sXS", [128, 2 * 512], F32))
        sSQ = E(nc.sbuf_tensor("sSQ", [128, 512], BF16))
        sXR = E(nc.sbuf_tensor("sXR", [128, 3 * CW], F32))
        sOS = E(nc.sbuf_tensor("sOS", [128, 2 * CW], F32))
        sSCR = E(nc.sbuf_tensor("sSCR", [1, 512], F32))
        sSCR2 = E(nc.sbuf_tensor("sSCR2", [1, 512], F32))
        sMHI = E(nc.sbuf_tensor("sMHI", [1, 512], BF16))
        sMLO = E(nc.sbuf_tensor("sMLO", [1, 512], BF16))
        sIDN = E(nc.sbuf_tensor("sIDN", [128, 128], F32))
        sR8 = E(nc.sbuf_tensor("sR8", [8, CW], BF16))
        sINVC = E(nc.sbuf_tensor("sINVC", [128, 16], F32))
        sINVD = E(nc.sbuf_tensor("sINVD", [128, 16], F32))
        sONC = E(nc.sbuf_tensor("sONC", [128, 1], BF16))
        sONR = E(nc.sbuf_tensor("sONR", [1, 128], BF16))

        pm = [E(nc.psum_tensor(f"pm{i}", [128, 512], F32)) for i in range(4)]
        pst = E(nc.psum_tensor("pst", [128, 512], F32))
        paux = E(nc.psum_tensor("paux", [128, 512], F32))
        pbm = E(nc.psum_tensor("pbm", [128, 512], F32))
        pA = E(nc.psum_tensor("pA", [128, 512], F32))
        BK = [pst, paux, pbm, pA]

        S = lambda name: _Sem(st, nc, name)
        sLx, sLk, sLs, sW, sMT = S("sLx"), S("sLk"), S("sLs"), S("sW"), S("sMT")
        sXre, sSt, sE40 = S("sXre"), S("sSt"), S("sE40")
        sMk = [S(f"sMk{i}") for i in range(4)]
        sXc, sSq, sTsq = S("sXc"), S("sSq"), S("sTsq")
        sVr, sSd2, sTiv, sIvc = S("sVr"), S("sSd2"), S("sTiv"), S("sIvc")
        sMh, sMb, sFin, sXh, sTer, sE1 = S("sMh"), S("sMb"), S("sFin"), S("sXh"), S("sTer"), S("sE1")
        sWs, sWp = S("sWs"), S("sWp")
        sWcR, sWcF, sPs, sEpi = S("sWcR"), S("sWcF"), S("sPs"), S("sEpi")
        sAmm, sAh, sBd = S("sAmm"), S("sAh"), S("sBd")
        sTr, sBt, sBit, sBx, sBm = S("sTr"), S("sBt"), S("sBit"), S("sBx"), S("sBm")
        sRd = S("sRd")

        blk = E(nc.Block())
        lkm = {}

        @blk.sync
        def _(sy):
            sLs.inc(sy.dma_start(sIDN[:, :], didn[:, :]), 16)
            for i in range(256):
                j, h8 = i // 8, i % 8
                if i >= 4:
                    sy.wait_ge(sXc.h, (i - 4) // 2 + 1)
                sLx.inc(sy.dma_start(
                    sXS[:, (i % 4) * 256:(i % 4) * 256 + 256],
                    dxT[j * 128:(j + 1) * 128, h8 * 256:(h8 + 1) * 256]), 16)
            for g in range(nch * NK):
                c, k = g // NK, g % NK
                if g >= RRAW:
                    sy.wait_ge(sWcR.h, g - RRAW + 1)
                sl = (g % RRAW) * 2 * CW
                sW.inc(sy.dma_start(
                    sRWB[:, sl:sl + 2 * CW],
                    dwwb[k * 128:(k + 1) * 128, c * 2 * CW:(c + 1) * 2 * CW]), 16)

        @blk.gpsimd
        def _(gp):
            ntile_max = 32 * (pmax + 1)

            def mt_issue(gp_, nt):
                sMT.inc(gp_.dma_start(
                    sMTb[:, (nt % MTS) * 512:(nt % MTS) * 512 + 512],
                    dMT[(nt % 32) * 128:(nt % 32 + 1) * 128,
                        (nt // 32) * 512:(nt // 32 + 1) * 512]), 16)

            npiece = 0
            for e in range(3):
                for h in range(2):
                    if npiece < min(MTS, ntile_max):
                        mt_issue(gp, npiece)
                    if npiece > 0:
                        gp.wait_ge(sWp.h, npiece)
                    src = dcoef0[e].rearrange("(j p) r -> p j r", p=128)[:, h * 16:(h + 1) * 16, :]
                    sLk.inc(gp.dma_start(
                        sXR[:, 0:512].rearrange("p (j r) -> p j r", r=32), src), 16)
                    srcm = dmsk0[e].rearrange("(j p) r -> p j r", p=128)[:, h * 16:(h + 1) * 16, :]
                    sLk.inc(gp.dma_start(
                        sOS[:, 0:512].rearrange("p (j r) -> p j r", r=32), srcm), 16)
                    lkm[("c0", e, h)] = sLk.n
                    npiece += 1
            for e in range(3):
                if npiece < min(MTS, ntile_max):
                    mt_issue(gp, npiece)
                gp.wait_ge(sWp.h, npiece)
                src = dcoef1[e].rearrange("(j p) r -> p j r", p=128)
                sLk.inc(gp.dma_start(
                    sXR[:, 0:256].rearrange("p (j r) -> p j r", r=8), src), 16)
                srcm = dmsk1[e].rearrange("(j p) r -> p j r", p=128)
                sLk.inc(gp.dma_start(
                    sOS[:, 0:256].rearrange("p (j r) -> p j r", r=8), srcm), 16)
                lkm[("c1", e)] = sLk.n
                npiece += 1
            for nt in range(9, min(MTS, ntile_max)):
                mt_issue(gp, nt)
            src = dencT.rearrange("(j p) r -> p j r", p=128)
            sLk.inc(gp.dma_start(sENC.rearrange("p (j r) -> p j r", r=48), src), 16)
            sLk.inc(gp.dma_start(sR8[:, :], dR8[:, :]), 16)
            lkm["consts"] = sLk.n

            def bias_dmas(p):
                if p >= 2:
                    gp.wait_ge(sPs.h, 16 * (2 * p - 2))
                hb = (p % 2) * 512
                sBd.inc(gp.dma_start(sAH[40:41, hb:hb + 512],
                                     dbA[0:1, p * 512:(p + 1) * 512]), 16)
                sBd.inc(gp.dma_start(sAH[41:42, hb:hb + 512],
                                     dbB[0:1, p * 512:(p + 1) * 512]), 16)

            def chains(g):
                for l in range(8):
                    for i in range(4):
                        jj = i * 8 + l
                        if l == 0:
                            if g == 0:
                                gp.wait_ge(sWs.h, 1)
                            gp.wait_ge(sRd.h, g)
                        else:
                            gp.wait_ge(sMk[i].h, 16 * (8 * g + l))
                        dst = sMP[:, i * CW:(i + 1) * CW]
                        src = dmkT[jj * 128:(jj + 1) * 128, g * CW:(g + 1) * CW]
                        if l == 0:
                            sMk[i].inc(gp.dma_start(dst, src), 16)
                        else:
                            sMk[i].inc(gp.dma_start(dst, src, accum_op=ALU.add), 16)

            if nch >= 1:
                chains(0)
            if nch >= 2:
                chains(1)
            for tg in range(4):
                gp.wait_ge(sSd2.h, tg + 1)
                for r in (40, 41):
                    sE40.inc(gp.dma_start(sE01[r:r + 1, tg * 512:(tg + 1) * 512],
                                          sSCR[0:1, :]), 16)
            for nt in range(min(MTS, ntile_max), min(32 + MTS - 1, ntile_max)):
                gp.wait_ge(sAmm.h, nt - (MTS - 1))
                mt_issue(gp, nt)
            if pmax >= 0:
                bias_dmas(0)
            for c in range(nch):
                if c + 2 < nch:
                    chains(c + 2)
                if c % 2 == 1 and (c + 1) // 2 <= pmax:
                    bias_dmas((c + 1) // 2)

        @blk.scalar
        def _(sc):
            for i in range(128):
                j, tg = i // 4, i % 4
                sc.wait_ge(sLx.h, 16 * min(2 * i + 4, 256))
                base = j * T + tg * 512
                sXc.inc(sc.copy(sXH[:, base:base + 512],
                                sXS[:, (2 * i % 4) * 256:(2 * i % 4) * 256 + 512]))
                if i > 0:
                    sc.wait_ge(sTsq.h, i)
                sSq.inc(sc.square(sSQ[:, :], sXH[:, base:base + 512]))
            for tg in range(4):
                sc.wait_ge(sVr.h, tg + 1)
                sSd2.inc(sc.activation(sSCR[0:1, :], sSCR2[0:1, :], AF.Sqrt))
                for sub in range(4):
                    sc.wait_ge(sTiv.h, 4 * tg + sub + 1)
                    sIvc.inc(sc.copy(sINVC[:, 4 * tg + sub:4 * tg + sub + 1],
                                     BK[tg][:, sub:sub + 1]))
            def sign_op(g):
                sc.wait_ge(sRd.h, g + 1)
                if g > 0:
                    sc.wait_ge(sBt.h, g)
                sTr.inc(sc.activation(sBM2[:, :], sBMI[:, :], AF.Sign))
            if nch >= 1:
                sign_op(0)
            for Q in range(-2, 16 * nch):
                if Q >= 0 and Q % NTS == 0 and Q // NTS + 1 < nch:
                    sign_op(Q // NTS + 1)
                Qx = Q + 2
                if Qx < 16 * nch:
                    cx, qx = Qx // NTS, Qx % NTS
                    if Qx < 2:
                        sc.wait_ge(sWp.h, 9)
                    if Qx >= 3:
                        sc.wait_ge(sEpi.h, Qx - 2)
                    sXre.inc(sc.dma_start(sXR[:, (Qx % 3) * CW:(Qx % 3) * CW + CW],
                                          dxtm[qx * 128:(qx + 1) * 128, cx * CW:(cx + 1) * CW]), 16)
                if Q >= 0:
                    c, q = Q // NTS, Q % NTS
                    sc.wait_ge(sEpi.h, Q + 1)
                    sSt.inc(sc.dma_start(dout[q * 128:(q + 1) * 128, c * CW:(c + 1) * CW],
                                         sOS[:, (Q % 2) * CW:(Q % 2) * CW + CW]), 16)
                    if 0 <= Q < 32 * (pmax + 1) - 31 - MTS:
                        nt = Q + 31 + MTS
                        sMT.inc(sc.dma_start(
                            sMTb[:, (nt % MTS) * 512:(nt % MTS) * 512 + 512],
                            dMT[(nt % 32) * 128:(nt % 32 + 1) * 128,
                                (nt // 32) * 512:(nt // 32 + 1) * 512]), 16)
            sc.wait_ge(sSt.h, 16 * 16 * nch)
            if dbg:
                sc.wait_ge(sXh.h, 32)
                sc.dma_start(dbg_xh[:, :], sXH[:, :]).then_inc(sSt.h, 16)
                sc.wait_ge(sE1.h, 4)
                sc.wait_ge(sE40.h, 128)
                sc.dma_start(dbg_e01[:, :], sE01[:, :]).then_inc(sSt.h, 16)
                sc.wait_ge(sFin.h, 4)
                sc.dma_start(dbg_mr[:, :], sWC[:, 0:T]).then_inc(sSt.h, 16)
                sc.dma_start(dbg_inv[:, :], sINVD[:, :]).then_inc(sSt.h, 16)
                sc.wait_ge(sWs.h, 1)
                sc.dma_start(dbg_ws[:, :], sWS[:, :]).then_inc(sSt.h, 16)
                if nch >= 1:
                    sc.wait_ge(sWcR.h, min(nch * NK, RING))
                    sc.dma_start(dbg_wc[:, :], sWC[:, :]).then_inc(sSt.h, 16)
                    sc.wait_ge(sAh.h, 1)
                    sc.wait_ge(sBd.h, 32)
                    sc.dma_start(dbg_ah[:, :], sAH[:, :]).then_inc(sSt.h, 16)
                    sc.wait_ge(sBm.h, 1)
                    sc.dma_start(dbg_bx[:, :], sBX[:, :]).then_inc(sSt.h, 16)
                    sc.wait_ge(sTr.h, 1)
                    sc.dma_start(dbg_mp[:, :], sMP[:, :]).then_inc(sSt.h, 16)
                    sc.dma_start(dbg_bmi[:, :], sBM2[:, :]).then_inc(sSt.h, 16)
                sc.wait_ge(sSt.h, 16 * 16 * nch + 16 * (10 if nch >= 1 else 5))

        @blk.vector
        def _(v):
            v.memset(sWS[:, :], 0.0)
            v.memset(sE01[32:48, :], 0.0)
            v.memset(sAH[32:48, :], 0.0)
            v.memset(sONC[:, :], 1.0)
            v.memset(sONR[:, :], 1.0)
            wtmp = sWC[:, 46 * CW:46 * CW + 512]
            npiece = 0
            for e in range(3):
                for h in range(2):
                    v.wait_ge(sLk.h, lkm[("c0", e, h)])
                    acc = sMP[:, h * 512:h * 512 + 512]
                    if e == 0:
                        ins = v.tensor_tensor(acc, sXR[:, 0:512], sOS[:, 0:512], ALU.mult)
                    else:
                        v.tensor_tensor(wtmp, sXR[:, 0:512], sOS[:, 0:512], ALU.mult)
                        ins = v.tensor_tensor(acc, acc, wtmp, ALU.add)
                    npiece += 1
                    sWp.inc(ins)
            for e in range(3):
                v.wait_ge(sLk.h, lkm[("c1", e)])
                if e == 0:
                    ins = v.tensor_tensor(sBS[:, 0:256], sXR[:, 0:256], sOS[:, 0:256], ALU.mult)
                else:
                    v.tensor_tensor(wtmp[:, 0:256], sXR[:, 0:256], sOS[:, 0:256], ALU.mult)
                    ins = v.tensor_tensor(sBS[:, 0:256], sBS[:, 0:256], wtmp[:, 0:256], ALU.add)
                npiece += 1
                sWp.inc(ins)
            for j in range(NK):
                jj = j % 16
                v.tensor_scalar(sWS[:, j * 48:j * 48 + 32],
                                sMP[:, (j // 16) * 512 + jj * 32:(j // 16) * 512 + jj * 32 + 32],
                                0.5, None, ALU.mult)
                ins = v.tensor_scalar(sWS[:, j * 48 + 32:j * 48 + 40],
                                      sBS[:, j * 8:j * 8 + 8], 0.5, None, ALU.mult)
                if j == NK - 1:
                    sWs.inc(ins)
            def front(g):
                for i in range(4):
                    v.wait_ge(sMk[i].h, 16 * 8 * (g + 1))
                v.tensor_tensor(sBS[:, :], sMP[:, 0:CW], sMP[:, CW:2 * CW], ALU.add)
                v.tensor_tensor(sMP[:, 2 * CW:3 * CW], sMP[:, 2 * CW:3 * CW],
                                sMP[:, 3 * CW:4 * CW], ALU.add)
                v.tensor_tensor(sBS[:, :], sBS[:, :], sMP[:, 2 * CW:3 * CW], ALU.add)
                if g > 0:
                    v.wait_ge(sTr.h, g)
                v.memset(sSPC[:, :], 0.0)
                sRd.inc(v.tensor_reduce(sBMI[:, :], sBS.rearrange("p (a b) -> p a b", b=32),
                                        AX.X, ALU.add))

            if nch >= 1:
                front(0)
            v.wait_ge(sTsq.h, 128)
            for tg in range(4):
                rm = pm[tg][0:1, 0:512]
                rq = BK[tg][0:1, 0:512]
                v.tensor_scalar(sSCR[0:1, :], rm, 1.0 / 4096.0, None, ALU.mult)
                v.tensor_scalar(sSCR2[0:1, :], rq, 1.0 / 4096.0, None, ALU.mult)
                v.tensor_tensor(rm, sSCR[0:1, :], sSCR[0:1, :], ALU.mult)
                v.tensor_copy(sMHI[0:1, :], sSCR[0:1, :])
                v.tensor_tensor(sSCR2[0:1, :], sSCR2[0:1, :], rm, ALU.subtract)
                v.tensor_tensor(rm, sSCR[0:1, :], sMHI[0:1, :], ALU.subtract)
                sVr.inc(v.tensor_scalar(sSCR2[0:1, :], sSCR2[0:1, :], 1e-5, None, ALU.add))
                sMh.inc(v.tensor_copy(sMLO[0:1, :], rm))
                v.wait_ge(sTiv.h, 4 * (tg + 1))
                v.wait_ge(sE40.h, 32 * (tg + 1))
                v.wait_ge(sMb.h, tg + 1)
                sFin.inc(v.tensor_copy(sWC[:, tg * 512:(tg + 1) * 512], BK[tg][:, 0:512]))
            v.reciprocal(sINVD[:, :], sINVC[:, :])
            for j in range(NK):
                v.wait_ge(sTsq.h, 4 * (j + 1))
                v.wait_ge(sFin.h, 4)
                sXh.inc(v.tensor_tensor(sXH[:, j * T:(j + 1) * T],
                                        sXH[:, j * T:(j + 1) * T], sWC[:, 0:T], ALU.subtract))
            for tg in range(4):
                v.wait_ge(sTer.h, tg + 1)
                sE1.inc(v.tensor_copy(sE01[0:40, tg * 512:(tg + 1) * 512],
                                      pm[tg][0:40, 0:512]))

            def bm_copies(g):
                v.wait_ge(sBt.h, g + 1)
                sBit.inc(v.tensor_copy(sBIT[0:8, 0:128], paux[0:8, 0:128]))
                v.wait_ge(sBx.h, g + 1)
                sBm.inc(v.tensor_copy(sBX[:, :], pbm[:, 0:CW]))

            def ahat_evac(p):
                v.wait_ge(sAmm.h, 32 * (p + 1))
                if p >= 2:
                    v.wait_ge(sPs.h, 16 * (2 * p - 2))
                hb = (p % 2) * 512
                sAh.inc(v.tensor_copy(sAH[0:40, hb:hb + 512], pA[0:40, 0:512]))

            def epi(Q):
                q = Q % NTS
                v.wait_ge(sPs.h, Q + 1)
                v.wait_ge(sXre.h, 16 * min(Q + 2, 16 * nch))
                if Q >= 1:
                    v.wait_ge(sSt.h, 16 * Q)
                sEpi.inc(v.scalar_tensor_tensor(
                    sOS[:, (Q % 2) * CW:(Q % 2) * CW + CW],
                    pm[Q % 4][:, 0:CW], sINVD[:, q:q + 1],
                    sXR[:, (Q % 3) * CW:(Q % 3) * CW + CW], ALU.mult, ALU.add))

            def combines(c, k0=0, k1=NK):
                for k in range(k0, k1):
                    g = c * NK + k
                    v.wait_ge(sW.h, min(16 * (g + 4), 16 * nch * NK))
                    if g >= RING:
                        v.wait_ge(sWcF.h, g + 1 - RING)
                    sl = (g % RRAW) * 2 * CW
                    v.tensor_tensor(pst[:, 0:CW], sRWB[:, sl:sl + CW], sBX[:, :], ALU.mult)
                    sWcR.inc(v.tensor_tensor(
                        sWC[:, (g % RING) * CW:(g % RING) * CW + CW],
                        pst[:, 0:CW], sRWB[:, sl + CW:sl + 2 * CW], ALU.add))

            if nch >= 1:
                bm_copies(0)
                v.wait_ge(sLk.h, lkm["consts"])
                ahat_evac(0)
                if noai:
                    for pp in range(1, pmax + 1):
                        ahat_evac(pp)
                combines(0)
            for c in range(nch):
                if c + 1 < nch:
                    front(c + 1)
                for q in range(0, 4):
                    epi(16 * c + q)
                if c + 1 < nch:
                    bm_copies(c + 1)
                for q in range(4, 16):
                    epi(16 * c + q)
                    if c + 1 < nch:
                        combines(c + 1, 2 * (q - 4), 2 * (q - 4) + 2)
                    if q == 12 and c % 2 == 1 and (c + 1) // 2 <= pmax and not noai:
                        ahat_evac((c + 1) // 2)
                if c + 1 < nch:
                    combines(c + 1, 24, NK)

        @blk.tensor
        def _(t):
            ntile_max = 32 * (pmax + 1)
            for i in range(128):
                j, tg = i // 4, i % 4
                base = j * T + tg * 512
                t.wait_ge(sXc.h, i + 1)
                t.matmul(pm[tg][0:1, 0:512], sONC[:, :], sXH[:, base:base + 512],
                         start=(j == 0), stop=(j == 31))
                t.wait_ge(sSq.h, i + 1)
                sTsq.inc(t.matmul(BK[tg][0:1, 0:512], sONC[:, :], sSQ[:, :],
                                  start=(j == 0), stop=(j == 31)))
            t.wait_ge(sLs.h, 16)
            for tg in range(4):
                t.wait_ge(sSd2.h, tg + 1)
                for sub in range(4):
                    sTiv.inc(t.matmul(BK[tg][:, sub:sub + 1],
                                      sSCR[0:1, sub * 128:sub * 128 + 128],
                                      sIDN[0:1, 0:1], start=True, stop=True))
                t.wait_ge(sIvc.h, 4 * (tg + 1))
                t.wait_ge(sMh.h, tg + 1)
                t.matmul(BK[tg][:, 0:512], sONR[:, :], sMHI[0:1, :],
                         start=True, stop=False)
                sMb.inc(t.matmul(BK[tg][:, 0:512], sONR[:, :], sMLO[0:1, :],
                                 start=False, stop=True))
            t.wait_ge(sFin.h, 4)
            t.wait_ge(sLk.h, lkm["consts"])
            if pmax >= 0:
                t.wait_ge(sWs.h, 1)
            for j in range(NK):
                t.wait_ge(sXh.h, j + 1)
                for tg in range(4):
                    mm = t.matmul(pm[tg][0:48, 0:512], sENC[:, j * 48:(j + 1) * 48],
                                  sXH[:, j * T + tg * 512:j * T + tg * 512 + 512],
                                  start=(j == 0), stop=(j == 31))
                    if j == 31:
                        sTer.inc(mm)
                if pmax >= 0:
                    t.wait_ge(sMT.h, 16 * min(j + 4, ntile_max))
                    sAmm.inc(t.matmul(pA[0:48, 0:512], sWS[:, j * 48:(j + 1) * 48],
                                      sMTb[:, (j % MTS) * 512:(j % MTS) * 512 + 512],
                                      start=(j == 0), stop=(j == 31)))
            def a_mms(pp):
                t.wait_ge(sAh.h, pp)
                for o in range(32):
                    gidx = 32 * pp + o
                    t.wait_ge(sMT.h, 16 * min(gidx + 4, ntile_max))
                    mm = t.matmul(pA[0:48, 0:512], sWS[:, o * 48:(o + 1) * 48],
                                  sMTb[:, (gidx % MTS) * 512:(gidx % MTS) * 512 + 512],
                                  start=(o == 0), stop=(o == 31))
                    sAmm.inc(mm)
            if nch >= 1:
                t.wait_ge(sTr.h, 1)
                sBt.inc(t.transpose(paux[0:8, 0:128], sBM2[:, :], sIDN[:, :]))
                t.wait_ge(sBit.h, 1)
                sBx.inc(t.matmul(pbm[:, 0:CW], sBIT[0:8, 0:128], sR8[0:8, 0:CW],
                                 start=True, stop=True))
            if noai:
                for pp in range(1, pmax + 1):
                    a_mms(pp)
            for c in range(nch):
                p2 = c // 2 + 1
                for ts in range(NTS):
                    Q = 16 * c + ts
                    if Q >= 4:
                        t.wait_ge(sEpi.h, Q - 3)
                    if ts == 0:
                        t.wait_ge(sAh.h, c // 2 + 1)
                        t.wait_ge(sBd.h, 32 * (c // 2 + 1))
                        if c == 0:
                            t.wait_ge(sE1.h, 4)
                            t.wait_ge(sE40.h, 128)
                    for k in range(NK):
                        g = c * NK + k
                        if ts == 0:
                            t.wait_ge(sWcR.h, g + 1)
                        mm = t.matmul(pm[Q % 4][:, 0:CW],
                                      sXH[:, k * T + ts * 128:k * T + ts * 128 + 128],
                                      sWC[:, (g % RING) * CW:(g % RING) * CW + CW],
                                      start=(k == 0), stop=False)
                        if ts == NTS - 1:
                            sWcF.inc(mm)
                    hb = ((c // 2) % 2) * 512 + (c % 2) * CW
                    sPs.inc(t.matmul(pm[Q % 4][:, 0:CW],
                                     sE01[0:48, ts * 128:(ts + 1) * 128],
                                     sAH[0:48, hb:hb + CW],
                                     start=False, stop=True))
                    if ts == 4 and c + 1 < nch:
                        t.wait_ge(sTr.h, c + 2)
                        sBt.inc(t.transpose(paux[0:8, 0:128], sBM2[:, :], sIDN[:, :]))
                    if ts == 5 and c + 1 < nch:
                        t.wait_ge(sBit.h, c + 2)
                        t.wait_ge(sBm.h, c + 1)
                        sBx.inc(t.matmul(pbm[:, 0:CW], sBIT[0:8, 0:128],
                                         sR8[0:8, 0:CW], start=True, stop=True))
                    if p2 <= pmax and not noai:
                        o = (c % 2) * 16 + ts
                        gidx = 32 * p2 + o
                        t.wait_ge(sMT.h, 16 * min(gidx + 4, ntile_max))
                        if o == 0:
                            t.wait_ge(sAh.h, p2)
                        mm = t.matmul(pA[0:48, 0:512], sWS[:, o * 48:(o + 1) * 48],
                                      sMTb[:, (gidx % MTS) * 512:(gidx % MTS) * 512 + 512],
                                      start=(o == 0), stop=(o == 31))
                        sAmm.inc(mm)

        if nch != NCH:
            _CACHE[key] = nc
            return nc
        assert sLx.n == 16 * 256 and sW.n == 16 * NCH * NK
        assert sMT.n == 16 * 256 and sXre.n == 16 * NQ and sSt.n == 16 * NQ
        assert all(sm.n == 16 * 8 * NCH for sm in sMk)
        assert sXc.n == 128 and sSq.n == 128 and sTsq.n == 128
        assert sVr.n == 4 and sSd2.n == 4 and sTiv.n == 16 and sIvc.n == 16
        assert sMh.n == 4 and sMb.n == 4 and sFin.n == 4 and sE40.n == 16 * 8
        assert sXh.n == NK and sTer.n == 4 and sE1.n == 4 and sWs.n == 1 and sWp.n == 9
        assert sWcR.n == NCH * NK and sWcF.n == NCH * NK and sPs.n == NQ and sEpi.n == NQ
        assert sAmm.n == 256 and sAh.n == 8 and sBd.n == 32 * 8
        assert sTr.n == NCH and sBt.n == NCH and sBit.n == NCH and sBx.n == NCH and sBm.n == NCH
        assert sRd.n == NCH

        _CACHE[key] = nc
    return nc


def kernel(L0_attn_data, L0_attn_mask, L0_attn_weight, L0_attn_bias,
           L0_ffn_W_base, L0_ffn_b_base, L0_ffn_enc0, L0_ffn_enc1,
           L0_ffn_coefs0, L0_ffn_coefs1, L0_ffn_mask0, L0_ffn_mask1):
    asc = np.ascontiguousarray
    x = asc(np.asarray(L0_attn_data, dtype=np.float32)).reshape(NCORES * T, D)

    _wt = _perm_rows(asc(np.asarray(L0_attn_weight, np.float32)).T).reshape(D, NCH, CW)
    _wb = _perm_rows(asc(np.asarray(L0_ffn_W_base, np.float32)).T).reshape(D, NCH, CW)
    wwbT = asc(np.concatenate([_wt, _wb], axis=2).reshape(D, 2 * D))
    m32 = asc(np.asarray(L0_attn_mask, np.int32).T)
    mkT = _perm_rows(asc(m32.view(np.uint8)[:, 0::4]))
    MT = _build_MT()
    encT = np.zeros((D, 48), dtype=np.float32)
    encT[:, 0:32] = np.asarray(L0_ffn_enc0, np.float32).T
    encT[:, 32:40] = np.asarray(L0_ffn_enc1, np.float32).T
    encT = _perm_rows(encT)
    coef0 = asc(np.asarray(L0_ffn_coefs0, np.float32))
    coef1 = asc(np.asarray(L0_ffn_coefs1, np.float32))
    msk0 = asc(np.asarray(L0_ffn_mask0)).view(np.uint8)
    msk1 = asc(np.asarray(L0_ffn_mask1)).view(np.uint8)
    biasA = asc(np.asarray(L0_attn_bias, np.float32)).reshape(1, D)
    biasB = asc(np.asarray(L0_ffn_b_base, np.float32)).reshape(1, D)
    idn = np.eye(128, dtype=np.float32)
    R8 = np.kron(np.eye(8, dtype=np.float32), np.ones((1, 32), dtype=np.float32))
    R8 = asc(R8)

    shared = dict(wwbT=wwbT, mkT=mkT, MT=MT, encT=encT,
                  coef0=coef0, msk0=msk0, coef1=coef1, msk1=msk1,
                  biasA=biasA, biasB=biasB, idn=idn, R8=R8)
    in_maps = []
    for ci in range(NCORES):
        xs = x[ci * T:(ci + 1) * T]
        m = dict(shared)
        m["xT"] = _perm_rows(asc(xs.T))
        m["xtm"] = asc(xs)
        in_maps.append(m)

    nc = _build_nc()
    res = run_bass_kernel_spmd(nc, in_maps, core_ids=list(range(NCORES)))
    out = np.stack([np.asarray(res.results[i]["out"], dtype=np.float32)
                    for i in range(NCORES)], axis=0)
    return out.reshape(NCORES, T, D)


# version 15 (final)
# speedup vs baseline: 1.0940x; 1.0244x over previous
import math
import contextlib
import numpy as np
import ml_dtypes
import concourse.bass as bass
import concourse.mybir as mybir
from concourse.bass_utils import run_bass_kernel_spmd

F32 = mybir.dt.float32
BF16 = mybir.dt.bfloat16
U8 = mybir.dt.uint8
AF = mybir.ActivationFunctionType
ALU = mybir.AluOpType
AX = mybir.AxisListType

NCORES = 8
T = 2048
D = 4096
NK = 32
NCH = 16
CW = 256
NTS = 16
NQ = NCH * NTS
RING = 54
RRAW = 5
MTS = 8


def _perm_rows(a):
    a = np.ascontiguousarray(a)
    return np.ascontiguousarray(
        a.reshape(128, 32, a.shape[1]).transpose(1, 0, 2).reshape(4096, a.shape[1]))


_CACHE = {}


def _build_MT():
    if "MT" not in _CACHE:
        n = 4096
        i = np.arange(n)[:, None].astype(np.float64)
        f = np.arange(1, n // 2)[None, :].astype(np.float64)
        ang = 2.0 * np.pi * i * f / n
        M = np.zeros((n, n), dtype=np.float64)
        M[:, 0] = 1.0 / math.sqrt(n)
        M[:, 1:n - 1:2] = math.sqrt(2.0 / n) * np.cos(ang)
        M[:, 2:n - 1:2] = math.sqrt(2.0 / n) * np.sin(ang)
        M[:, n - 1] = ((-1.0) ** np.arange(n)) / math.sqrt(n)
        _CACHE["MT"] = np.ascontiguousarray(M.T.astype(np.float32)).astype(ml_dtypes.bfloat16)
    return _CACHE["MT"]


class _Sem:
    __slots__ = ("h", "n", "name")

    def __init__(self, stack, nc, name):
        self.h = stack.enter_context(nc.semaphore(name))
        self.n = 0
        self.name = name

    def inc(self, instr, k=1):
        instr.then_inc(self.h, k)
        self.n += k
        return instr


def _build_nc(nch=NCH, dbg=False, noai=False):
    key = ("nc", nch, dbg, noai)
    if key in _CACHE:
        return _CACHE[key]
    pmax = (nch - 1) // 2 if nch >= 1 else -1
    nc = bass.Bass("TRN2", target_bir_lowering=False)

    dxT = nc.dram_tensor("xT", [D, T], F32, kind="ExternalInput").ap()
    dxtm = nc.dram_tensor("xtm", [T, D], F32, kind="ExternalInput").ap()
    dwwb = nc.dram_tensor("wwbT", [D, 2 * D], F32, kind="ExternalInput").ap()
    dmkT = nc.dram_tensor("mkT", [D, D], U8, kind="ExternalInput").ap()
    dMT = nc.dram_tensor("MT", [D, D], BF16, kind="ExternalInput").ap()
    dencT = nc.dram_tensor("encT", [D, 48], F32, kind="ExternalInput").ap()
    dcoef0 = nc.dram_tensor("coef0", [3, D, 32], F32, kind="ExternalInput").ap()
    dmsk0 = nc.dram_tensor("msk0", [3, D, 32], U8, kind="ExternalInput").ap()
    dcoef1 = nc.dram_tensor("coef1", [3, D, 8], F32, kind="ExternalInput").ap()
    dmsk1 = nc.dram_tensor("msk1", [3, D, 8], U8, kind="ExternalInput").ap()
    dbA = nc.dram_tensor("biasA", [1, D], F32, kind="ExternalInput").ap()
    dbB = nc.dram_tensor("biasB", [1, D], F32, kind="ExternalInput").ap()
    didn = nc.dram_tensor("idn", [128, 128], F32, kind="ExternalInput").ap()
    dR8 = nc.dram_tensor("R8", [8, CW], F32, kind="ExternalInput").ap()
    dout = nc.dram_tensor("out", [T, D], F32, kind="ExternalOutput").ap()
    if dbg:
        dbg_xh = nc.dram_tensor("dbg_xh", [128, NK * T], BF16, kind="ExternalOutput").ap()
        dbg_e01 = nc.dram_tensor("dbg_e01", [48, T], BF16, kind="ExternalOutput").ap()
        dbg_mr = nc.dram_tensor("dbg_mr", [128, T], BF16, kind="ExternalOutput").ap()
        dbg_inv = nc.dram_tensor("dbg_inv", [128, 16], F32, kind="ExternalOutput").ap()
        dbg_ws = nc.dram_tensor("dbg_ws", [128, NK * 48], BF16, kind="ExternalOutput").ap()
        dbg_wc = nc.dram_tensor("dbg_wc", [128, RING * CW], BF16, kind="ExternalOutput").ap()
        dbg_ah = nc.dram_tensor("dbg_ah", [48, 2 * 512], BF16, kind="ExternalOutput").ap()
        dbg_bx = nc.dram_tensor("dbg_bx", [128, CW], BF16, kind="ExternalOutput").ap()
        dbg_mp = nc.dram_tensor("dbg_mp", [128, 4 * CW], BF16, kind="ExternalOutput").ap()
        dbg_bmi = nc.dram_tensor("dbg_bmi", [128, 8], F32, kind="ExternalOutput").ap()

    with contextlib.ExitStack() as st:
        E = st.enter_context
        sXH = E(nc.sbuf_tensor("sXH", [128, NK * T], BF16))
        sWC = E(nc.sbuf_tensor("sWC", [128, RING * CW], BF16))
        sRWB = E(nc.sbuf_tensor("sRWB", [128, RRAW * 2 * CW], F32))
        sMP = E(nc.sbuf_tensor("sMP", [128, 4 * CW], BF16))
        sBS = E(nc.sbuf_tensor("sBS", [128, CW], F32))
        sBMI = E(nc.sbuf_tensor("sBMI", [128, 8], F32))
        sBM2 = E(nc.sbuf_tensor("sBM2", [128, 8], F32))
        sSPC = E(nc.sbuf_tensor("sSPC", [128, 2], F32))
        sBIT = E(nc.sbuf_tensor("sBIT", [16, 128], BF16))
        sBX = E(nc.sbuf_tensor("sBX", [128, CW], BF16))
        sMTb = E(nc.sbuf_tensor("sMTb", [128, MTS * 512], BF16))
        sAH = E(nc.sbuf_tensor("sAH", [48, 2 * 512], BF16))
        sE01 = E(nc.sbuf_tensor("sE01", [48, T], BF16))
        sENC = E(nc.sbuf_tensor("sENC", [128, NK * 48], BF16))
        sWS = E(nc.sbuf_tensor("sWS", [128, NK * 48], BF16))
        sXS = E(nc.sbuf_tensor("sXS", [128, 2 * 512], F32))
        sSQ = E(nc.sbuf_tensor("sSQ", [128, 2 * 512], BF16))
        sXR = E(nc.sbuf_tensor("sXR", [128, 3 * CW], F32))
        sOS = E(nc.sbuf_tensor("sOS", [128, 2 * CW], F32))
        sSCR = E(nc.sbuf_tensor("sSCR", [1, 512], F32))
        sSCR2 = E(nc.sbuf_tensor("sSCR2", [1, 512], F32))
        sMHI = E(nc.sbuf_tensor("sMHI", [1, 512], BF16))
        sMLO = E(nc.sbuf_tensor("sMLO", [1, 512], BF16))
        sIDN = E(nc.sbuf_tensor("sIDN", [128, 128], F32))
        sR8 = E(nc.sbuf_tensor("sR8", [8, CW], BF16))
        sINVC = E(nc.sbuf_tensor("sINVC", [128, 16], F32))
        sINVD = E(nc.sbuf_tensor("sINVD", [128, 16], F32))
        sONC = E(nc.sbuf_tensor("sONC", [128, 1], BF16))
        sONR = E(nc.sbuf_tensor("sONR", [1, 128], BF16))

        pm = [E(nc.psum_tensor(f"pm{i}", [128, 512], F32)) for i in range(4)]
        pst = E(nc.psum_tensor("pst", [128, 512], F32))
        PSB = pm + [pst]
        paux = E(nc.psum_tensor("paux", [128, 512], F32))
        pbm = E(nc.psum_tensor("pbm", [128, 512], F32))
        pA = E(nc.psum_tensor("pA", [128, 512], F32))
        BK = [pst, paux, pbm, pA]

        S = lambda name: _Sem(st, nc, name)
        sLx, sLk, sLs, sW, sMT = S("sLx"), S("sLk"), S("sLs"), S("sW"), S("sMT")
        sXre, sSt, sE40 = S("sXre"), S("sSt"), S("sE40")
        sMk = [S(f"sMk{i}") for i in range(4)]
        sXc, sSq, sTsq = S("sXc"), S("sSq"), S("sTsq")
        sVr, sSd2, sTiv, sIvc = S("sVr"), S("sSd2"), S("sTiv"), S("sIvc")
        sMh, sMb, sFin, sXh, sTer, sE1 = S("sMh"), S("sMb"), S("sFin"), S("sXh"), S("sTer"), S("sE1")
        sWs, sWp = S("sWs"), S("sWp")
        sWcR, sWcF, sPs, sEpi = S("sWcR"), S("sWcF"), S("sPs"), S("sEpi")
        sAmm, sAh, sBd = S("sAmm"), S("sAh"), S("sBd")
        sTr, sBt, sBit, sBx, sBm = S("sTr"), S("sBt"), S("sBit"), S("sBx"), S("sBm")
        sRd = S("sRd")

        blk = E(nc.Block())
        lkm = {}

        @blk.sync
        def _(sy):
            sLs.inc(sy.dma_start(sIDN[:, :], didn[:, :]), 16)
            for i in range(256):
                j, h8 = i // 8, i % 8
                if i >= 4:
                    sy.wait_ge(sXc.h, (i - 4) // 2 + 1)
                sLx.inc(sy.dma_start(
                    sXS[:, (i % 4) * 256:(i % 4) * 256 + 256],
                    dxT[j * 128:(j + 1) * 128, h8 * 256:(h8 + 1) * 256]), 16)
            for g in range(nch * NK):
                c, k = g // NK, g % NK
                if g >= RRAW:
                    sy.wait_ge(sWcR.h, g - RRAW + 1)
                sl = (g % RRAW) * 2 * CW
                sW.inc(sy.dma_start(
                    sRWB[:, sl:sl + 2 * CW],
                    dwwb[k * 128:(k + 1) * 128, c * 2 * CW:(c + 1) * 2 * CW]), 16)

        @blk.gpsimd
        def _(gp):
            ntile_max = 32 * (pmax + 1)

            def mt_issue(gp_, nt):
                sMT.inc(gp_.dma_start(
                    sMTb[:, (nt % MTS) * 512:(nt % MTS) * 512 + 512],
                    dMT[(nt % 32) * 128:(nt % 32 + 1) * 128,
                        (nt // 32) * 512:(nt // 32 + 1) * 512]), 16)

            npiece = 0
            for e in range(3):
                for h in range(2):
                    if npiece < min(MTS, ntile_max):
                        mt_issue(gp, npiece)
                    if npiece > 0:
                        gp.wait_ge(sWp.h, npiece)
                    src = dcoef0[e].rearrange("(j p) r -> p j r", p=128)[:, h * 16:(h + 1) * 16, :]
                    sLk.inc(gp.dma_start(
                        sXR[:, 0:512].rearrange("p (j r) -> p j r", r=32), src), 16)
                    srcm = dmsk0[e].rearrange("(j p) r -> p j r", p=128)[:, h * 16:(h + 1) * 16, :]
                    sLk.inc(gp.dma_start(
                        sOS[:, 0:512].rearrange("p (j r) -> p j r", r=32), srcm), 16)
                    lkm[("c0", e, h)] = sLk.n
                    npiece += 1
            for e in range(3):
                if npiece < min(MTS, ntile_max):
                    mt_issue(gp, npiece)
                gp.wait_ge(sWp.h, npiece)
                src = dcoef1[e].rearrange("(j p) r -> p j r", p=128)
                sLk.inc(gp.dma_start(
                    sXR[:, 0:256].rearrange("p (j r) -> p j r", r=8), src), 16)
                srcm = dmsk1[e].rearrange("(j p) r -> p j r", p=128)
                sLk.inc(gp.dma_start(
                    sOS[:, 0:256].rearrange("p (j r) -> p j r", r=8), srcm), 16)
                lkm[("c1", e)] = sLk.n
                npiece += 1
            for nt in range(9, min(MTS, ntile_max)):
                mt_issue(gp, nt)
            src = dencT.rearrange("(j p) r -> p j r", p=128)
            sLk.inc(gp.dma_start(sENC.rearrange("p (j r) -> p j r", r=48), src), 16)
            sLk.inc(gp.dma_start(sR8[:, :], dR8[:, :]), 16)
            lkm["consts"] = sLk.n

            def bias_dmas(p):
                if p >= 2:
                    gp.wait_ge(sPs.h, 16 * (2 * p - 2))
                hb = (p % 2) * 512
                sBd.inc(gp.dma_start(sAH[40:41, hb:hb + 512],
                                     dbA[0:1, p * 512:(p + 1) * 512]), 16)
                sBd.inc(gp.dma_start(sAH[41:42, hb:hb + 512],
                                     dbB[0:1, p * 512:(p + 1) * 512]), 16)

            def chains(g):
                for l in range(8):
                    for i in range(4):
                        jj = i * 8 + l
                        if l == 0:
                            if g == 0:
                                gp.wait_ge(sWs.h, 1)
                            gp.wait_ge(sRd.h, g)
                        else:
                            gp.wait_ge(sMk[i].h, 16 * (8 * g + l))
                        dst = sMP[:, i * CW:(i + 1) * CW]
                        src = dmkT[jj * 128:(jj + 1) * 128, g * CW:(g + 1) * CW]
                        if l == 0:
                            sMk[i].inc(gp.dma_start(dst, src), 16)
                        else:
                            sMk[i].inc(gp.dma_start(dst, src, accum_op=ALU.add), 16)

            if nch >= 1:
                chains(0)
            if nch >= 2:
                chains(1)
            for tg in range(4):
                gp.wait_ge(sSd2.h, tg + 1)
                for r in (40, 41):
                    sE40.inc(gp.dma_start(sE01[r:r + 1, tg * 512:(tg + 1) * 512],
                                          sSCR[0:1, :]), 16)
            for nt in range(min(MTS, ntile_max), min(32 + MTS - 1, ntile_max)):
                gp.wait_ge(sAmm.h, nt - (MTS - 1))
                mt_issue(gp, nt)
            if pmax >= 0:
                bias_dmas(0)
            for c in range(nch):
                if c + 2 < nch:
                    chains(c + 2)
                if c % 2 == 1 and (c + 1) // 2 <= pmax:
                    bias_dmas((c + 1) // 2)

        @blk.scalar
        def _(sc):
            for i in range(128):
                j, tg = i // 4, i % 4
                sc.wait_ge(sLx.h, 16 * min(2 * i + 4, 256))
                base = j * T + tg * 512
                sXc.inc(sc.copy(sXH[:, base:base + 512],
                                sXS[:, (2 * i % 4) * 256:(2 * i % 4) * 256 + 512]))
                if i > 1:
                    sc.wait_ge(sTsq.h, i - 1)
                sSq.inc(sc.square(sSQ[:, (i % 2) * 512:(i % 2) * 512 + 512],
                                  sXH[:, base:base + 512]))
            for tg in range(4):
                sc.wait_ge(sVr.h, tg + 1)
                sSd2.inc(sc.activation(sSCR[0:1, :], sSCR2[0:1, :], AF.Sqrt))
                for sub in range(4):
                    sc.wait_ge(sTiv.h, 4 * tg + sub + 1)
                    sIvc.inc(sc.copy(sINVC[:, 4 * tg + sub:4 * tg + sub + 1],
                                     BK[tg][:, sub:sub + 1]))
            def sign_op(g):
                sc.wait_ge(sRd.h, g + 1)
                if g > 0:
                    sc.wait_ge(sBt.h, g)
                sTr.inc(sc.activation(sBM2[:, :], sBMI[:, :], AF.Sign))
            if nch >= 1:
                sign_op(0)
            for Q in range(-2, 16 * nch):
                if Q >= 0 and Q % NTS == 0 and Q // NTS + 1 < nch:
                    sign_op(Q // NTS + 1)
                Qx = Q + 2
                if Qx < 16 * nch:
                    cx, qx = Qx // NTS, Qx % NTS
                    if Qx < 2:
                        sc.wait_ge(sWp.h, 9)
                    if Qx >= 3:
                        sc.wait_ge(sEpi.h, Qx - 2)
                    sXre.inc(sc.dma_start(sXR[:, (Qx % 3) * CW:(Qx % 3) * CW + CW],
                                          dxtm[qx * 128:(qx + 1) * 128, cx * CW:(cx + 1) * CW]), 16)
                if Q >= 0:
                    c, q = Q // NTS, Q % NTS
                    sc.wait_ge(sEpi.h, Q + 1)
                    sSt.inc(sc.dma_start(dout[q * 128:(q + 1) * 128, c * CW:(c + 1) * CW],
                                         sOS[:, (Q % 2) * CW:(Q % 2) * CW + CW]), 16)
                    if 0 <= Q < 32 * (pmax + 1) - 31 - MTS:
                        nt = Q + 31 + MTS
                        sMT.inc(sc.dma_start(
                            sMTb[:, (nt % MTS) * 512:(nt % MTS) * 512 + 512],
                            dMT[(nt % 32) * 128:(nt % 32 + 1) * 128,
                                (nt // 32) * 512:(nt // 32 + 1) * 512]), 16)
            sc.wait_ge(sSt.h, 16 * 16 * nch)
            if dbg:
                sc.wait_ge(sXh.h, 32)
                sc.dma_start(dbg_xh[:, :], sXH[:, :]).then_inc(sSt.h, 16)
                sc.wait_ge(sE1.h, 4)
                sc.wait_ge(sE40.h, 128)
                sc.dma_start(dbg_e01[:, :], sE01[:, :]).then_inc(sSt.h, 16)
                sc.wait_ge(sFin.h, 4)
                sc.dma_start(dbg_mr[:, :], sWC[:, 0:T]).then_inc(sSt.h, 16)
                sc.dma_start(dbg_inv[:, :], sINVD[:, :]).then_inc(sSt.h, 16)
                sc.wait_ge(sWs.h, 1)
                sc.dma_start(dbg_ws[:, :], sWS[:, :]).then_inc(sSt.h, 16)
                if nch >= 1:
                    sc.wait_ge(sWcR.h, min(nch * NK, RING))
                    sc.dma_start(dbg_wc[:, :], sWC[:, :]).then_inc(sSt.h, 16)
                    sc.wait_ge(sAh.h, 1)
                    sc.wait_ge(sBd.h, 32)
                    sc.dma_start(dbg_ah[:, :], sAH[:, :]).then_inc(sSt.h, 16)
                    sc.wait_ge(sBm.h, 1)
                    sc.dma_start(dbg_bx[:, :], sBX[:, :]).then_inc(sSt.h, 16)
                    sc.wait_ge(sTr.h, 1)
                    sc.dma_start(dbg_mp[:, :], sMP[:, :]).then_inc(sSt.h, 16)
                    sc.dma_start(dbg_bmi[:, :], sBM2[:, :]).then_inc(sSt.h, 16)
                sc.wait_ge(sSt.h, 16 * 16 * nch + 16 * (10 if nch >= 1 else 5))

        @blk.vector
        def _(v):
            v.memset(sWS[:, :], 0.0)
            v.memset(sE01[32:48, :], 0.0)
            v.memset(sAH[32:48, :], 0.0)
            v.memset(sONC[:, :], 1.0)
            v.memset(sONR[:, :], 1.0)
            wtmp = sWC[:, 46 * CW:46 * CW + 512]
            npiece = 0
            for e in range(3):
                for h in range(2):
                    v.wait_ge(sLk.h, lkm[("c0", e, h)])
                    acc = sMP[:, h * 512:h * 512 + 512]
                    if e == 0:
                        ins = v.tensor_tensor(acc, sXR[:, 0:512], sOS[:, 0:512], ALU.mult)
                    else:
                        v.tensor_tensor(wtmp, sXR[:, 0:512], sOS[:, 0:512], ALU.mult)
                        ins = v.tensor_tensor(acc, acc, wtmp, ALU.add)
                    npiece += 1
                    sWp.inc(ins)
            for e in range(3):
                v.wait_ge(sLk.h, lkm[("c1", e)])
                if e == 0:
                    ins = v.tensor_tensor(sBS[:, 0:256], sXR[:, 0:256], sOS[:, 0:256], ALU.mult)
                else:
                    v.tensor_tensor(wtmp[:, 0:256], sXR[:, 0:256], sOS[:, 0:256], ALU.mult)
                    ins = v.tensor_tensor(sBS[:, 0:256], sBS[:, 0:256], wtmp[:, 0:256], ALU.add)
                npiece += 1
                sWp.inc(ins)
            for j in range(NK):
                jj = j % 16
                v.tensor_scalar(sWS[:, j * 48:j * 48 + 32],
                                sMP[:, (j // 16) * 512 + jj * 32:(j // 16) * 512 + jj * 32 + 32],
                                0.5, None, ALU.mult)
                ins = v.tensor_scalar(sWS[:, j * 48 + 32:j * 48 + 40],
                                      sBS[:, j * 8:j * 8 + 8], 0.5, None, ALU.mult)
                if j == NK - 1:
                    sWs.inc(ins)
            def front(g):
                for i in range(4):
                    v.wait_ge(sMk[i].h, 16 * 8 * (g + 1))
                v.tensor_tensor(sBS[:, :], sMP[:, 0:CW], sMP[:, CW:2 * CW], ALU.add)
                v.tensor_tensor(sMP[:, 2 * CW:3 * CW], sMP[:, 2 * CW:3 * CW],
                                sMP[:, 3 * CW:4 * CW], ALU.add)
                v.tensor_tensor(sBS[:, :], sBS[:, :], sMP[:, 2 * CW:3 * CW], ALU.add)
                if g > 0:
                    v.wait_ge(sTr.h, g)
                v.memset(sSPC[:, :], 0.0)
                sRd.inc(v.tensor_reduce(sBMI[:, :], sBS.rearrange("p (a b) -> p a b", b=32),
                                        AX.X, ALU.add))

            if nch >= 1:
                front(0)
            v.wait_ge(sTsq.h, 128)
            for tg in range(4):
                rm = pm[tg][0:1, 0:512]
                rq = BK[tg][0:1, 0:512]
                v.tensor_scalar(sSCR[0:1, :], rm, 1.0 / 4096.0, None, ALU.mult)
                v.tensor_scalar(sSCR2[0:1, :], rq, 1.0 / 4096.0, None, ALU.mult)
                v.tensor_tensor(rm, sSCR[0:1, :], sSCR[0:1, :], ALU.mult)
                v.tensor_copy(sMHI[0:1, :], sSCR[0:1, :])
                v.tensor_tensor(sSCR2[0:1, :], sSCR2[0:1, :], rm, ALU.subtract)
                v.tensor_tensor(rm, sSCR[0:1, :], sMHI[0:1, :], ALU.subtract)
                sVr.inc(v.tensor_scalar(sSCR2[0:1, :], sSCR2[0:1, :], 1e-5, None, ALU.add))
                sMh.inc(v.tensor_copy(sMLO[0:1, :], rm))
                v.wait_ge(sTiv.h, 4 * (tg + 1))
                v.wait_ge(sE40.h, 32 * (tg + 1))
                v.wait_ge(sMb.h, tg + 1)
                sFin.inc(v.tensor_copy(sWC[:, tg * 512:(tg + 1) * 512], BK[tg][:, 0:512]))
            v.reciprocal(sINVD[:, :], sINVC[:, :])
            for j in range(NK):
                v.wait_ge(sTsq.h, 4 * (j + 1))
                v.wait_ge(sFin.h, 4)
                sXh.inc(v.tensor_tensor(sXH[:, j * T:(j + 1) * T],
                                        sXH[:, j * T:(j + 1) * T], sWC[:, 0:T], ALU.subtract))
            for tg in range(4):
                v.wait_ge(sTer.h, tg + 1)
                sE1.inc(v.tensor_copy(sE01[0:40, tg * 512:(tg + 1) * 512],
                                      pm[tg][0:40, 0:512]))

            def bm_copies(g):
                v.wait_ge(sBt.h, g + 1)
                sBit.inc(v.tensor_copy(sBIT[0:8, 0:128], paux[0:8, 0:128]))
                v.wait_ge(sBx.h, g + 1)
                sBm.inc(v.tensor_copy(sBX[:, :], pbm[:, 0:CW]))

            def ahat_evac(p):
                v.wait_ge(sAmm.h, 32 * (p + 1))
                if p >= 2:
                    v.wait_ge(sPs.h, 16 * (2 * p - 2))
                hb = (p % 2) * 512
                sAh.inc(v.tensor_copy(sAH[0:40, hb:hb + 512], pA[0:40, 0:512]))

            def epi(Q):
                q = Q % NTS
                v.wait_ge(sPs.h, Q + 1)
                v.wait_ge(sXre.h, 16 * min(Q + 2, 16 * nch))
                if Q >= 1:
                    v.wait_ge(sSt.h, 16 * Q)
                sEpi.inc(v.scalar_tensor_tensor(
                    sOS[:, (Q % 2) * CW:(Q % 2) * CW + CW],
                    PSB[Q % 5][:, 0:CW], sINVD[:, q:q + 1],
                    sXR[:, (Q % 3) * CW:(Q % 3) * CW + CW], ALU.mult, ALU.add))

            def combines(c, k0=0, k1=NK):
                for k in range(k0, k1):
                    g = c * NK + k
                    v.wait_ge(sW.h, min(16 * (g + 4), 16 * nch * NK))
                    if g >= RING:
                        v.wait_ge(sWcF.h, g + 1 - RING)
                    sl = (g % RRAW) * 2 * CW
                    v.tensor_tensor(sSQ[:, 0:CW], sRWB[:, sl:sl + CW], sBX[:, :], ALU.mult)
                    sWcR.inc(v.tensor_tensor(
                        sWC[:, (g % RING) * CW:(g % RING) * CW + CW],
                        sSQ[:, 0:CW], sRWB[:, sl + CW:sl + 2 * CW], ALU.add))

            if nch >= 1:
                bm_copies(0)
                v.wait_ge(sLk.h, lkm["consts"])
                ahat_evac(0)
                if noai:
                    for pp in range(1, pmax + 1):
                        ahat_evac(pp)
                combines(0)
            for c in range(nch):
                if c + 1 < nch:
                    front(c + 1)
                for q in range(0, 4):
                    epi(16 * c + q)
                if c + 1 < nch:
                    bm_copies(c + 1)
                for q in range(4, 16):
                    epi(16 * c + q)
                    if c + 1 < nch:
                        combines(c + 1, 2 * (q - 4), 2 * (q - 4) + 2)
                    if q == 12 and c % 2 == 1 and (c + 1) // 2 <= pmax and not noai:
                        ahat_evac((c + 1) // 2)
                if c + 1 < nch:
                    combines(c + 1, 24, NK)

        @blk.tensor
        def _(t):
            ntile_max = 32 * (pmax + 1)
            for i in range(128):
                j, tg = i // 4, i % 4
                base = j * T + tg * 512
                t.wait_ge(sXc.h, i + 1)
                t.matmul(pm[tg][0:1, 0:512], sONC[:, :], sXH[:, base:base + 512],
                         start=(j == 0), stop=(j == 31))
                t.wait_ge(sSq.h, i + 1)
                sTsq.inc(t.matmul(BK[tg][0:1, 0:512], sONC[:, :],
                                  sSQ[:, (i % 2) * 512:(i % 2) * 512 + 512],
                                  start=(j == 0), stop=(j == 31)))
            t.wait_ge(sLs.h, 16)
            for tg in range(4):
                t.wait_ge(sSd2.h, tg + 1)
                for sub in range(4):
                    sTiv.inc(t.matmul(BK[tg][:, sub:sub + 1],
                                      sSCR[0:1, sub * 128:sub * 128 + 128],
                                      sIDN[0:1, 0:1], start=True, stop=True))
                t.wait_ge(sIvc.h, 4 * (tg + 1))
                t.wait_ge(sMh.h, tg + 1)
                t.matmul(BK[tg][:, 0:512], sONR[:, :], sMHI[0:1, :],
                         start=True, stop=False)
                sMb.inc(t.matmul(BK[tg][:, 0:512], sONR[:, :], sMLO[0:1, :],
                                 start=False, stop=True))
            t.wait_ge(sFin.h, 4)
            t.wait_ge(sLk.h, lkm["consts"])
            if pmax >= 0:
                t.wait_ge(sWs.h, 1)
            for j in range(NK):
                t.wait_ge(sXh.h, j + 1)
                for tg in range(4):
                    mm = t.matmul(pm[tg][0:48, 0:512], sENC[:, j * 48:(j + 1) * 48],
                                  sXH[:, j * T + tg * 512:j * T + tg * 512 + 512],
                                  start=(j == 0), stop=(j == 31))
                    if j == 31:
                        sTer.inc(mm)
                if pmax >= 0:
                    t.wait_ge(sMT.h, 16 * min(j + 4, ntile_max))
                    sAmm.inc(t.matmul(pA[0:48, 0:512], sWS[:, j * 48:(j + 1) * 48],
                                      sMTb[:, (j % MTS) * 512:(j % MTS) * 512 + 512],
                                      start=(j == 0), stop=(j == 31)))
            def a_mms(pp):
                t.wait_ge(sAh.h, pp)
                for o in range(32):
                    gidx = 32 * pp + o
                    t.wait_ge(sMT.h, 16 * min(gidx + 4, ntile_max))
                    mm = t.matmul(pA[0:48, 0:512], sWS[:, o * 48:(o + 1) * 48],
                                  sMTb[:, (gidx % MTS) * 512:(gidx % MTS) * 512 + 512],
                                  start=(o == 0), stop=(o == 31))
                    sAmm.inc(mm)
            if nch >= 1:
                t.wait_ge(sTr.h, 1)
                sBt.inc(t.transpose(paux[0:8, 0:128], sBM2[:, :], sIDN[:, :]))
                t.wait_ge(sBit.h, 1)
                sBx.inc(t.matmul(pbm[:, 0:CW], sBIT[0:8, 0:128], sR8[0:8, 0:CW],
                                 start=True, stop=True))
            if noai:
                for pp in range(1, pmax + 1):
                    a_mms(pp)
            for c in range(nch):
                p2 = c // 2 + 1
                for ts in range(NTS):
                    Q = 16 * c + ts
                    if Q >= 5:
                        t.wait_ge(sEpi.h, Q - 4)
                    if ts == 0:
                        t.wait_ge(sAh.h, c // 2 + 1)
                        t.wait_ge(sBd.h, 32 * (c // 2 + 1))
                        if c == 0:
                            t.wait_ge(sE1.h, 4)
                            t.wait_ge(sE40.h, 128)
                    for k in range(NK):
                        g = c * NK + k
                        if ts == 0:
                            t.wait_ge(sWcR.h, g + 1)
                        mm = t.matmul(PSB[Q % 5][:, 0:CW],
                                      sXH[:, k * T + ts * 128:k * T + ts * 128 + 128],
                                      sWC[:, (g % RING) * CW:(g % RING) * CW + CW],
                                      start=(k == 0), stop=False)
                        if ts == NTS - 1:
                            sWcF.inc(mm)
                    hb = ((c // 2) % 2) * 512 + (c % 2) * CW
                    sPs.inc(t.matmul(PSB[Q % 5][:, 0:CW],
                                     sE01[0:48, ts * 128:(ts + 1) * 128],
                                     sAH[0:48, hb:hb + CW],
                                     start=False, stop=True))
                    if ts == 4 and c + 1 < nch:
                        t.wait_ge(sTr.h, c + 2)
                        sBt.inc(t.transpose(paux[0:8, 0:128], sBM2[:, :], sIDN[:, :]))
                    if ts == 5 and c + 1 < nch:
                        t.wait_ge(sBit.h, c + 2)
                        t.wait_ge(sBm.h, c + 1)
                        sBx.inc(t.matmul(pbm[:, 0:CW], sBIT[0:8, 0:128],
                                         sR8[0:8, 0:CW], start=True, stop=True))
                    if p2 <= pmax and not noai:
                        o = (c % 2) * 16 + ts
                        gidx = 32 * p2 + o
                        t.wait_ge(sMT.h, 16 * min(gidx + 4, ntile_max))
                        if o == 0:
                            t.wait_ge(sAh.h, p2)
                        mm = t.matmul(pA[0:48, 0:512], sWS[:, o * 48:(o + 1) * 48],
                                      sMTb[:, (gidx % MTS) * 512:(gidx % MTS) * 512 + 512],
                                      start=(o == 0), stop=(o == 31))
                        sAmm.inc(mm)

        if nch != NCH:
            _CACHE[key] = nc
            return nc
        assert sLx.n == 16 * 256 and sW.n == 16 * NCH * NK
        assert sMT.n == 16 * 256 and sXre.n == 16 * NQ and sSt.n == 16 * NQ
        assert all(sm.n == 16 * 8 * NCH for sm in sMk)
        assert sXc.n == 128 and sSq.n == 128 and sTsq.n == 128
        assert sVr.n == 4 and sSd2.n == 4 and sTiv.n == 16 and sIvc.n == 16
        assert sMh.n == 4 and sMb.n == 4 and sFin.n == 4 and sE40.n == 16 * 8
        assert sXh.n == NK and sTer.n == 4 and sE1.n == 4 and sWs.n == 1 and sWp.n == 9
        assert sWcR.n == NCH * NK and sWcF.n == NCH * NK and sPs.n == NQ and sEpi.n == NQ
        assert sAmm.n == 256 and sAh.n == 8 and sBd.n == 32 * 8
        assert sTr.n == NCH and sBt.n == NCH and sBit.n == NCH and sBx.n == NCH and sBm.n == NCH
        assert sRd.n == NCH

        _CACHE[key] = nc
    return nc


def kernel(L0_attn_data, L0_attn_mask, L0_attn_weight, L0_attn_bias,
           L0_ffn_W_base, L0_ffn_b_base, L0_ffn_enc0, L0_ffn_enc1,
           L0_ffn_coefs0, L0_ffn_coefs1, L0_ffn_mask0, L0_ffn_mask1):
    asc = np.ascontiguousarray
    x = asc(np.asarray(L0_attn_data, dtype=np.float32)).reshape(NCORES * T, D)

    _wt = _perm_rows(asc(np.asarray(L0_attn_weight, np.float32)).T).reshape(D, NCH, CW)
    _wb = _perm_rows(asc(np.asarray(L0_ffn_W_base, np.float32)).T).reshape(D, NCH, CW)
    wwbT = asc(np.concatenate([_wt, _wb], axis=2).reshape(D, 2 * D))
    m32 = asc(np.asarray(L0_attn_mask, np.int32).T)
    mkT = _perm_rows(asc(m32.view(np.uint8)[:, 0::4]))
    MT = _build_MT()
    encT = np.zeros((D, 48), dtype=np.float32)
    encT[:, 0:32] = np.asarray(L0_ffn_enc0, np.float32).T
    encT[:, 32:40] = np.asarray(L0_ffn_enc1, np.float32).T
    encT = _perm_rows(encT)
    coef0 = asc(np.asarray(L0_ffn_coefs0, np.float32))
    coef1 = asc(np.asarray(L0_ffn_coefs1, np.float32))
    msk0 = asc(np.asarray(L0_ffn_mask0)).view(np.uint8)
    msk1 = asc(np.asarray(L0_ffn_mask1)).view(np.uint8)
    biasA = asc(np.asarray(L0_attn_bias, np.float32)).reshape(1, D)
    biasB = asc(np.asarray(L0_ffn_b_base, np.float32)).reshape(1, D)
    idn = np.eye(128, dtype=np.float32)
    R8 = np.kron(np.eye(8, dtype=np.float32), np.ones((1, 32), dtype=np.float32))
    R8 = asc(R8)

    shared = dict(wwbT=wwbT, mkT=mkT, MT=MT, encT=encT,
                  coef0=coef0, msk0=msk0, coef1=coef1, msk1=msk1,
                  biasA=biasA, biasB=biasB, idn=idn, R8=R8)
    in_maps = []
    for ci in range(NCORES):
        xs = x[ci * T:(ci + 1) * T]
        m = dict(shared)
        m["xT"] = _perm_rows(asc(xs.T))
        m["xtm"] = asc(xs)
        in_maps.append(m)

    nc = _build_nc()
    res = run_bass_kernel_spmd(nc, in_maps, core_ids=list(range(NCORES)))
    out = np.stack([np.asarray(res.results[i]["out"], dtype=np.float32)
                    for i in range(NCORES)], axis=0)
    return out.reshape(NCORES, T, D)


# version 16
# speedup vs baseline: 1.1075x; 1.0123x over previous
import math
import contextlib
import numpy as np
import ml_dtypes
import concourse.bass as bass
import concourse.mybir as mybir
from concourse.bass_utils import run_bass_kernel_spmd

F32 = mybir.dt.float32
BF16 = mybir.dt.bfloat16
U8 = mybir.dt.uint8
AF = mybir.ActivationFunctionType
ALU = mybir.AluOpType
AX = mybir.AxisListType

NCORES = 8
T = 2048
D = 4096
NK = 32
NCH = 16
CW = 256
NTS = 16
NQ = NCH * NTS
RING = 54
RRAW = 5
MTS = 8


def _perm_rows(a):
    a = np.ascontiguousarray(a)
    return np.ascontiguousarray(
        a.reshape(128, 32, a.shape[1]).transpose(1, 0, 2).reshape(4096, a.shape[1]))


_CACHE = {}


def _build_MT():
    if "MT" not in _CACHE:
        n = 4096
        i = np.arange(n)[:, None].astype(np.float64)
        f = np.arange(1, n // 2)[None, :].astype(np.float64)
        ang = 2.0 * np.pi * i * f / n
        M = np.zeros((n, n), dtype=np.float64)
        M[:, 0] = 1.0 / math.sqrt(n)
        M[:, 1:n - 1:2] = math.sqrt(2.0 / n) * np.cos(ang)
        M[:, 2:n - 1:2] = math.sqrt(2.0 / n) * np.sin(ang)
        M[:, n - 1] = ((-1.0) ** np.arange(n)) / math.sqrt(n)
        _CACHE["MT"] = np.ascontiguousarray(M.T.astype(np.float32)).astype(ml_dtypes.bfloat16)
    return _CACHE["MT"]


class _Sem:
    __slots__ = ("h", "n", "name")

    def __init__(self, stack, nc, name):
        self.h = stack.enter_context(nc.semaphore(name))
        self.n = 0
        self.name = name

    def inc(self, instr, k=1):
        instr.then_inc(self.h, k)
        self.n += k
        return instr


def _build_nc(nch=NCH, dbg=False, noai=False):
    key = ("nc", nch, dbg, noai)
    if key in _CACHE:
        return _CACHE[key]
    pmax = (nch - 1) // 2 if nch >= 1 else -1
    nc = bass.Bass("TRN2", target_bir_lowering=False)

    dxT = nc.dram_tensor("xT", [D, T], F32, kind="ExternalInput").ap()
    dxtm = nc.dram_tensor("xtm", [T, D], F32, kind="ExternalInput").ap()
    dwwb = nc.dram_tensor("wwbT", [D, 2 * D], F32, kind="ExternalInput").ap()
    dmkT = nc.dram_tensor("mkT", [D, D], U8, kind="ExternalInput").ap()
    dMT = nc.dram_tensor("MT", [D, D], BF16, kind="ExternalInput").ap()
    dencT = nc.dram_tensor("encT", [D, 48], F32, kind="ExternalInput").ap()
    dcoef0 = nc.dram_tensor("coef0", [3, D, 32], F32, kind="ExternalInput").ap()
    dmsk0 = nc.dram_tensor("msk0", [3, D, 32], U8, kind="ExternalInput").ap()
    dcoef1 = nc.dram_tensor("coef1", [3, D, 8], F32, kind="ExternalInput").ap()
    dmsk1 = nc.dram_tensor("msk1", [3, D, 8], U8, kind="ExternalInput").ap()
    dbA = nc.dram_tensor("biasA", [1, D], F32, kind="ExternalInput").ap()
    dbB = nc.dram_tensor("biasB", [1, D], F32, kind="ExternalInput").ap()
    didn = nc.dram_tensor("idn", [128, 128], F32, kind="ExternalInput").ap()
    dR8 = nc.dram_tensor("R8", [8, CW], F32, kind="ExternalInput").ap()
    dout = nc.dram_tensor("out", [T, D], F32, kind="ExternalOutput").ap()
    if dbg:
        dbg_xh = nc.dram_tensor("dbg_xh", [128, NK * T], BF16, kind="ExternalOutput").ap()
        dbg_e01 = nc.dram_tensor("dbg_e01", [48, T], BF16, kind="ExternalOutput").ap()
        dbg_mr = nc.dram_tensor("dbg_mr", [128, T], BF16, kind="ExternalOutput").ap()
        dbg_inv = nc.dram_tensor("dbg_inv", [128, 16], F32, kind="ExternalOutput").ap()
        dbg_ws = nc.dram_tensor("dbg_ws", [128, NK * 48], BF16, kind="ExternalOutput").ap()
        dbg_wc = nc.dram_tensor("dbg_wc", [128, RING * CW], BF16, kind="ExternalOutput").ap()
        dbg_ah = nc.dram_tensor("dbg_ah", [48, 2 * 512], BF16, kind="ExternalOutput").ap()
        dbg_bx = nc.dram_tensor("dbg_bx", [128, CW], BF16, kind="ExternalOutput").ap()
        dbg_mp = nc.dram_tensor("dbg_mp", [128, 4 * CW], BF16, kind="ExternalOutput").ap()
        dbg_bmi = nc.dram_tensor("dbg_bmi", [128, 8], F32, kind="ExternalOutput").ap()

    with contextlib.ExitStack() as st:
        E = st.enter_context
        sXH = E(nc.sbuf_tensor("sXH", [128, NK * T], BF16))
        sWC = E(nc.sbuf_tensor("sWC", [128, RING * CW], BF16))
        sRWB = E(nc.sbuf_tensor("sRWB", [128, RRAW * 2 * CW], F32))
        sMP = E(nc.sbuf_tensor("sMP", [128, 4 * CW], BF16))
        sBS = E(nc.sbuf_tensor("sBS", [128, CW], F32))
        sBMI = E(nc.sbuf_tensor("sBMI", [128, 8], F32))
        sBM2 = E(nc.sbuf_tensor("sBM2", [128, 8], F32))
        sSPC = E(nc.sbuf_tensor("sSPC", [128, 2], F32))
        sBIT = E(nc.sbuf_tensor("sBIT", [16, 128], BF16))
        sBX = E(nc.sbuf_tensor("sBX", [128, CW], BF16))
        sMTb = E(nc.sbuf_tensor("sMTb", [128, MTS * 512], BF16))
        sAH = E(nc.sbuf_tensor("sAH", [48, 2 * 512], BF16))
        sE01 = E(nc.sbuf_tensor("sE01", [48, T], BF16))
        sENC = E(nc.sbuf_tensor("sENC", [128, NK * 48], BF16))
        sWS = E(nc.sbuf_tensor("sWS", [128, NK * 48], BF16))
        sXS = E(nc.sbuf_tensor("sXS", [128, 2 * 512], F32))
        sSQ = E(nc.sbuf_tensor("sSQ", [128, 2 * 512], BF16))
        sXR = E(nc.sbuf_tensor("sXR", [128, 3 * CW], F32))
        sOS = E(nc.sbuf_tensor("sOS", [128, 2 * CW], F32))
        sSCR = E(nc.sbuf_tensor("sSCR", [1, 512], F32))
        sSCR2 = E(nc.sbuf_tensor("sSCR2", [1, 512], F32))
        sMHI = E(nc.sbuf_tensor("sMHI", [1, 512], BF16))
        sMLO = E(nc.sbuf_tensor("sMLO", [1, 512], BF16))
        sIDN = E(nc.sbuf_tensor("sIDN", [128, 128], F32))
        sR8 = E(nc.sbuf_tensor("sR8", [8, CW], BF16))
        sINVC = E(nc.sbuf_tensor("sINVC", [128, 16], F32))
        sINVD = E(nc.sbuf_tensor("sINVD", [128, 16], F32))
        sONC = E(nc.sbuf_tensor("sONC", [128, 1], BF16))
        sONR = E(nc.sbuf_tensor("sONR", [1, 128], BF16))

        pm = [E(nc.psum_tensor(f"pm{i}", [128, 512], F32)) for i in range(4)]
        pst = E(nc.psum_tensor("pst", [128, 512], F32))
        PSB = pm + [pst]
        paux = E(nc.psum_tensor("paux", [128, 512], F32))
        pbm = E(nc.psum_tensor("pbm", [128, 512], F32))
        pA = E(nc.psum_tensor("pA", [128, 512], F32))
        BK = [pst, paux, pbm, pA]

        S = lambda name: _Sem(st, nc, name)
        sLx, sLk, sLs, sW, sMT = S("sLx"), S("sLk"), S("sLs"), S("sW"), S("sMT")
        sXre, sSt, sE40 = S("sXre"), S("sSt"), S("sE40")
        sMk = [S(f"sMk{i}") for i in range(4)]
        sXc, sSq, sTsq = S("sXc"), S("sSq"), S("sTsq")
        sVr, sSd2, sTiv, sIvc = S("sVr"), S("sSd2"), S("sTiv"), S("sIvc")
        sMh, sMb, sFin, sXh, sTer, sE1 = S("sMh"), S("sMb"), S("sFin"), S("sXh"), S("sTer"), S("sE1")
        sWs, sWp = S("sWs"), S("sWp")
        sWcR, sWcF, sPs, sEpi = S("sWcR"), S("sWcF"), S("sPs"), S("sEpi")
        sAmm, sAh, sBd = S("sAmm"), S("sAh"), S("sBd")
        sTr, sBt, sBit, sBx, sBm = S("sTr"), S("sBt"), S("sBit"), S("sBx"), S("sBm")
        sRd = S("sRd")

        blk = E(nc.Block())
        lkm = {}

        @blk.sync
        def _(sy):
            sLs.inc(sy.dma_start(sIDN[:, :], didn[:, :]), 16)
            for k in range(512):
                j, h16 = k // 16, k % 16
                if k >= 8:
                    sy.wait_ge(sXc.h, (k - 8) // 4 + 1)
                sLx.inc(sy.dma_start(
                    sXS[:, (k % 8) * 128:(k % 8) * 128 + 128],
                    dxT[j * 128:(j + 1) * 128, h16 * 128:(h16 + 1) * 128]), 16)
            for g in range(nch * NK):
                c, k = g // NK, g % NK
                if g >= RRAW:
                    sy.wait_ge(sWcR.h, g - RRAW + 1)
                sl = (g % RRAW) * 2 * CW
                sW.inc(sy.dma_start(
                    sRWB[:, sl:sl + 2 * CW],
                    dwwb[k * 128:(k + 1) * 128, c * 2 * CW:(c + 1) * 2 * CW]), 16)

        @blk.gpsimd
        def _(gp):
            ntile_max = 32 * (pmax + 1)

            def mt_issue(gp_, nt):
                sMT.inc(gp_.dma_start(
                    sMTb[:, (nt % MTS) * 512:(nt % MTS) * 512 + 512],
                    dMT[(nt % 32) * 128:(nt % 32 + 1) * 128,
                        (nt // 32) * 512:(nt // 32 + 1) * 512]), 16)

            npiece = 0
            for e in range(3):
                for h in range(2):
                    if npiece < min(MTS, ntile_max):
                        mt_issue(gp, npiece)
                    if npiece > 0:
                        gp.wait_ge(sWp.h, npiece)
                    src = dcoef0[e].rearrange("(j p) r -> p j r", p=128)[:, h * 16:(h + 1) * 16, :]
                    sLk.inc(gp.dma_start(
                        sXR[:, 0:512].rearrange("p (j r) -> p j r", r=32), src), 16)
                    srcm = dmsk0[e].rearrange("(j p) r -> p j r", p=128)[:, h * 16:(h + 1) * 16, :]
                    sLk.inc(gp.dma_start(
                        sOS[:, 0:512].rearrange("p (j r) -> p j r", r=32), srcm), 16)
                    lkm[("c0", e, h)] = sLk.n
                    npiece += 1
            for e in range(3):
                if npiece < min(MTS, ntile_max):
                    mt_issue(gp, npiece)
                gp.wait_ge(sWp.h, npiece)
                src = dcoef1[e].rearrange("(j p) r -> p j r", p=128)
                sLk.inc(gp.dma_start(
                    sXR[:, 0:256].rearrange("p (j r) -> p j r", r=8), src), 16)
                srcm = dmsk1[e].rearrange("(j p) r -> p j r", p=128)
                sLk.inc(gp.dma_start(
                    sOS[:, 0:256].rearrange("p (j r) -> p j r", r=8), srcm), 16)
                lkm[("c1", e)] = sLk.n
                npiece += 1
            for nt in range(9, min(MTS, ntile_max)):
                mt_issue(gp, nt)
            src = dencT.rearrange("(j p) r -> p j r", p=128)
            sLk.inc(gp.dma_start(sENC.rearrange("p (j r) -> p j r", r=48), src), 16)
            sLk.inc(gp.dma_start(sR8[:, :], dR8[:, :]), 16)
            lkm["consts"] = sLk.n

            def bias_dmas(p):
                if p >= 2:
                    gp.wait_ge(sPs.h, 16 * (2 * p - 2))
                hb = (p % 2) * 512
                sBd.inc(gp.dma_start(sAH[40:41, hb:hb + 512],
                                     dbA[0:1, p * 512:(p + 1) * 512]), 16)
                sBd.inc(gp.dma_start(sAH[41:42, hb:hb + 512],
                                     dbB[0:1, p * 512:(p + 1) * 512]), 16)

            def chains(g):
                for l in range(8):
                    for i in range(4):
                        jj = i * 8 + l
                        if l == 0:
                            if g == 0:
                                gp.wait_ge(sWs.h, 1)
                            gp.wait_ge(sRd.h, g)
                        else:
                            gp.wait_ge(sMk[i].h, 16 * (8 * g + l))
                        dst = sMP[:, i * CW:(i + 1) * CW]
                        src = dmkT[jj * 128:(jj + 1) * 128, g * CW:(g + 1) * CW]
                        if l == 0:
                            sMk[i].inc(gp.dma_start(dst, src), 16)
                        else:
                            sMk[i].inc(gp.dma_start(dst, src, accum_op=ALU.add), 16)

            if nch >= 1:
                chains(0)
            if nch >= 2:
                chains(1)
            for tg in range(4):
                gp.wait_ge(sSd2.h, tg + 1)
                for r in (40, 41):
                    sE40.inc(gp.dma_start(sE01[r:r + 1, tg * 512:(tg + 1) * 512],
                                          sSCR[0:1, :]), 16)
            for nt in range(min(MTS, ntile_max), min(32 + MTS - 1, ntile_max)):
                gp.wait_ge(sAmm.h, nt - (MTS - 1))
                mt_issue(gp, nt)
            if pmax >= 0:
                bias_dmas(0)
            for c in range(nch):
                if c + 2 < nch:
                    chains(c + 2)
                if c % 2 == 1 and (c + 1) // 2 <= pmax:
                    bias_dmas((c + 1) // 2)

        @blk.scalar
        def _(sc):
            for i in range(128):
                j, tg = i // 4, i % 4
                sc.wait_ge(sLx.h, 16 * min(4 * i + 6, 512))
                base = j * T + tg * 512
                sXc.inc(sc.copy(sXH[:, base:base + 512],
                                sXS[:, (4 * i % 8) * 128:(4 * i % 8) * 128 + 512]))
                if i > 1:
                    sc.wait_ge(sTsq.h, i - 1)
                sSq.inc(sc.square(sSQ[:, (i % 2) * 512:(i % 2) * 512 + 512],
                                  sXH[:, base:base + 512]))
            for tg in range(4):
                sc.wait_ge(sVr.h, tg + 1)
                sSd2.inc(sc.activation(sSCR[0:1, :], sSCR2[0:1, :], AF.Sqrt))
                for sub in range(4):
                    sc.wait_ge(sTiv.h, 4 * tg + sub + 1)
                    sIvc.inc(sc.copy(sINVC[:, 4 * tg + sub:4 * tg + sub + 1],
                                     BK[tg][:, sub:sub + 1]))
            def sign_op(g):
                sc.wait_ge(sRd.h, g + 1)
                if g > 0:
                    sc.wait_ge(sBt.h, g)
                sTr.inc(sc.activation(sBM2[:, :], sBMI[:, :], AF.Sign))
            if nch >= 1:
                sign_op(0)
            for Q in range(-2, 16 * nch):
                if Q >= 0 and Q % NTS == 0 and Q // NTS + 1 < nch:
                    sign_op(Q // NTS + 1)
                Qx = Q + 2
                if Qx < 16 * nch:
                    cx, qx = Qx // NTS, Qx % NTS
                    if Qx < 2:
                        sc.wait_ge(sWp.h, 9)
                    if Qx >= 3:
                        sc.wait_ge(sEpi.h, Qx - 2)
                    sXre.inc(sc.dma_start(sXR[:, (Qx % 3) * CW:(Qx % 3) * CW + CW],
                                          dxtm[qx * 128:(qx + 1) * 128, cx * CW:(cx + 1) * CW]), 16)
                if Q >= 0:
                    c, q = Q // NTS, Q % NTS
                    sc.wait_ge(sEpi.h, Q + 1)
                    sSt.inc(sc.dma_start(dout[q * 128:(q + 1) * 128, c * CW:(c + 1) * CW],
                                         sOS[:, (Q % 2) * CW:(Q % 2) * CW + CW]), 16)
                    if 0 <= Q < 32 * (pmax + 1) - 31 - MTS:
                        nt = Q + 31 + MTS
                        sMT.inc(sc.dma_start(
                            sMTb[:, (nt % MTS) * 512:(nt % MTS) * 512 + 512],
                            dMT[(nt % 32) * 128:(nt % 32 + 1) * 128,
                                (nt // 32) * 512:(nt // 32 + 1) * 512]), 16)
            sc.wait_ge(sSt.h, 16 * 16 * nch)
            if dbg:
                sc.wait_ge(sXh.h, 32)
                sc.dma_start(dbg_xh[:, :], sXH[:, :]).then_inc(sSt.h, 16)
                sc.wait_ge(sE1.h, 4)
                sc.wait_ge(sE40.h, 128)
                sc.dma_start(dbg_e01[:, :], sE01[:, :]).then_inc(sSt.h, 16)
                sc.wait_ge(sFin.h, 4)
                sc.dma_start(dbg_mr[:, :], sWC[:, 0:T]).then_inc(sSt.h, 16)
                sc.dma_start(dbg_inv[:, :], sINVD[:, :]).then_inc(sSt.h, 16)
                sc.wait_ge(sWs.h, 1)
                sc.dma_start(dbg_ws[:, :], sWS[:, :]).then_inc(sSt.h, 16)
                if nch >= 1:
                    sc.wait_ge(sWcR.h, min(nch * NK, RING))
                    sc.dma_start(dbg_wc[:, :], sWC[:, :]).then_inc(sSt.h, 16)
                    sc.wait_ge(sAh.h, 1)
                    sc.wait_ge(sBd.h, 32)
                    sc.dma_start(dbg_ah[:, :], sAH[:, :]).then_inc(sSt.h, 16)
                    sc.wait_ge(sBm.h, 1)
                    sc.dma_start(dbg_bx[:, :], sBX[:, :]).then_inc(sSt.h, 16)
                    sc.wait_ge(sTr.h, 1)
                    sc.dma_start(dbg_mp[:, :], sMP[:, :]).then_inc(sSt.h, 16)
                    sc.dma_start(dbg_bmi[:, :], sBM2[:, :]).then_inc(sSt.h, 16)
                sc.wait_ge(sSt.h, 16 * 16 * nch + 16 * (10 if nch >= 1 else 5))

        @blk.vector
        def _(v):
            v.memset(sWS[:, :], 0.0)
            v.memset(sE01[32:48, :], 0.0)
            v.memset(sAH[32:48, :], 0.0)
            v.memset(sONC[:, :], 1.0)
            v.memset(sONR[:, :], 1.0)
            wtmp = sWC[:, 46 * CW:46 * CW + 512]
            npiece = 0
            for e in range(3):
                for h in range(2):
                    v.wait_ge(sLk.h, lkm[("c0", e, h)])
                    acc = sMP[:, h * 512:h * 512 + 512]
                    if e == 0:
                        ins = v.tensor_tensor(acc, sXR[:, 0:512], sOS[:, 0:512], ALU.mult)
                    else:
                        v.tensor_tensor(wtmp, sXR[:, 0:512], sOS[:, 0:512], ALU.mult)
                        ins = v.tensor_tensor(acc, acc, wtmp, ALU.add)
                    npiece += 1
                    sWp.inc(ins)
            for e in range(3):
                v.wait_ge(sLk.h, lkm[("c1", e)])
                if e == 0:
                    ins = v.tensor_tensor(sBS[:, 0:256], sXR[:, 0:256], sOS[:, 0:256], ALU.mult)
                else:
                    v.tensor_tensor(wtmp[:, 0:256], sXR[:, 0:256], sOS[:, 0:256], ALU.mult)
                    ins = v.tensor_tensor(sBS[:, 0:256], sBS[:, 0:256], wtmp[:, 0:256], ALU.add)
                npiece += 1
                sWp.inc(ins)
            for j in range(NK):
                jj = j % 16
                v.tensor_scalar(sWS[:, j * 48:j * 48 + 32],
                                sMP[:, (j // 16) * 512 + jj * 32:(j // 16) * 512 + jj * 32 + 32],
                                0.5, None, ALU.mult)
                ins = v.tensor_scalar(sWS[:, j * 48 + 32:j * 48 + 40],
                                      sBS[:, j * 8:j * 8 + 8], 0.5, None, ALU.mult)
                if j == NK - 1:
                    sWs.inc(ins)
            def front(g):
                for i in range(4):
                    v.wait_ge(sMk[i].h, 16 * 8 * (g + 1))
                v.tensor_tensor(sBS[:, :], sMP[:, 0:CW], sMP[:, CW:2 * CW], ALU.add)
                v.tensor_tensor(sMP[:, 2 * CW:3 * CW], sMP[:, 2 * CW:3 * CW],
                                sMP[:, 3 * CW:4 * CW], ALU.add)
                v.tensor_tensor(sBS[:, :], sBS[:, :], sMP[:, 2 * CW:3 * CW], ALU.add)
                if g > 0:
                    v.wait_ge(sTr.h, g)
                v.memset(sSPC[:, :], 0.0)
                sRd.inc(v.tensor_reduce(sBMI[:, :], sBS.rearrange("p (a b) -> p a b", b=32),
                                        AX.X, ALU.add))

            if nch >= 1:
                front(0)
            v.wait_ge(sTsq.h, 128)
            for tg in range(4):
                rm = pm[tg][0:1, 0:512]
                rq = BK[tg][0:1, 0:512]
                v.tensor_scalar(sSCR[0:1, :], rm, 1.0 / 4096.0, None, ALU.mult)
                v.tensor_scalar(sSCR2[0:1, :], rq, 1.0 / 4096.0, None, ALU.mult)
                v.tensor_tensor(rm, sSCR[0:1, :], sSCR[0:1, :], ALU.mult)
                v.tensor_copy(sMHI[0:1, :], sSCR[0:1, :])
                v.tensor_tensor(sSCR2[0:1, :], sSCR2[0:1, :], rm, ALU.subtract)
                v.tensor_tensor(rm, sSCR[0:1, :], sMHI[0:1, :], ALU.subtract)
                sVr.inc(v.tensor_scalar(sSCR2[0:1, :], sSCR2[0:1, :], 1e-5, None, ALU.add))
                sMh.inc(v.tensor_copy(sMLO[0:1, :], rm))
                v.wait_ge(sTiv.h, 4 * (tg + 1))
                v.wait_ge(sE40.h, 32 * (tg + 1))
                v.wait_ge(sMb.h, tg + 1)
                sFin.inc(v.tensor_copy(sWC[:, tg * 512:(tg + 1) * 512], BK[tg][:, 0:512]))
            v.reciprocal(sINVD[:, :], sINVC[:, :])
            for j in range(NK):
                v.wait_ge(sTsq.h, 4 * (j + 1))
                v.wait_ge(sFin.h, 4)
                sXh.inc(v.tensor_tensor(sXH[:, j * T:(j + 1) * T],
                                        sXH[:, j * T:(j + 1) * T], sWC[:, 0:T], ALU.subtract))
            for tg in range(4):
                v.wait_ge(sTer.h, tg + 1)
                sE1.inc(v.tensor_copy(sE01[0:40, tg * 512:(tg + 1) * 512],
                                      pm[tg][0:40, 0:512]))

            def bm_copies(g):
                v.wait_ge(sBt.h, g + 1)
                sBit.inc(v.tensor_copy(sBIT[0:8, 0:128], paux[0:8, 0:128]))
                v.wait_ge(sBx.h, g + 1)
                sBm.inc(v.tensor_copy(sBX[:, :], pbm[:, 0:CW]))

            def ahat_evac(p):
                v.wait_ge(sAmm.h, 32 * (p + 1))
                if p >= 2:
                    v.wait_ge(sPs.h, 16 * (2 * p - 2))
                hb = (p % 2) * 512
                sAh.inc(v.tensor_copy(sAH[0:40, hb:hb + 512], pA[0:40, 0:512]))

            def epi(Q):
                q = Q % NTS
                v.wait_ge(sPs.h, Q + 1)
                v.wait_ge(sXre.h, 16 * min(Q + 2, 16 * nch))
                if Q >= 1:
                    v.wait_ge(sSt.h, 16 * Q)
                sEpi.inc(v.scalar_tensor_tensor(
                    sOS[:, (Q % 2) * CW:(Q % 2) * CW + CW],
                    PSB[Q % 5][:, 0:CW], sINVD[:, q:q + 1],
                    sXR[:, (Q % 3) * CW:(Q % 3) * CW + CW], ALU.mult, ALU.add))

            def combines(c, k0=0, k1=NK):
                for k in range(k0, k1):
                    g = c * NK + k
                    v.wait_ge(sW.h, min(16 * (g + 4), 16 * nch * NK))
                    if g >= RING:
                        v.wait_ge(sWcF.h, g + 1 - RING)
                    sl = (g % RRAW) * 2 * CW
                    v.tensor_tensor(sSQ[:, 0:CW], sRWB[:, sl:sl + CW], sBX[:, :], ALU.mult)
                    sWcR.inc(v.tensor_tensor(
                        sWC[:, (g % RING) * CW:(g % RING) * CW + CW],
                        sSQ[:, 0:CW], sRWB[:, sl + CW:sl + 2 * CW], ALU.add))

            if nch >= 1:
                bm_copies(0)
                v.wait_ge(sLk.h, lkm["consts"])
                ahat_evac(0)
                if noai:
                    for pp in range(1, pmax + 1):
                        ahat_evac(pp)
                combines(0)
            for c in range(nch):
                if c + 1 < nch:
                    front(c + 1)
                for q in range(0, 4):
                    epi(16 * c + q)
                if c + 1 < nch:
                    bm_copies(c + 1)
                for q in range(4, 16):
                    epi(16 * c + q)
                    if c + 1 < nch:
                        combines(c + 1, 2 * (q - 4), 2 * (q - 4) + 2)
                    if q == 12 and c % 2 == 1 and (c + 1) // 2 <= pmax and not noai:
                        ahat_evac((c + 1) // 2)
                if c + 1 < nch:
                    combines(c + 1, 24, NK)

        @blk.tensor
        def _(t):
            ntile_max = 32 * (pmax + 1)
            for i in range(128):
                j, tg = i // 4, i % 4
                base = j * T + tg * 512
                t.wait_ge(sXc.h, i + 1)
                t.matmul(pm[tg][0:1, 0:512], sONC[:, :], sXH[:, base:base + 512],
                         start=(j == 0), stop=(j == 31))
                t.wait_ge(sSq.h, i + 1)
                sTsq.inc(t.matmul(BK[tg][0:1, 0:512], sONC[:, :],
                                  sSQ[:, (i % 2) * 512:(i % 2) * 512 + 512],
                                  start=(j == 0), stop=(j == 31)))
            t.wait_ge(sLs.h, 16)
            for tg in range(4):
                t.wait_ge(sSd2.h, tg + 1)
                for sub in range(4):
                    sTiv.inc(t.matmul(BK[tg][:, sub:sub + 1],
                                      sSCR[0:1, sub * 128:sub * 128 + 128],
                                      sIDN[0:1, 0:1], start=True, stop=True))
                t.wait_ge(sIvc.h, 4 * (tg + 1))
                t.wait_ge(sMh.h, tg + 1)
                t.matmul(BK[tg][:, 0:512], sONR[:, :], sMHI[0:1, :],
                         start=True, stop=False)
                sMb.inc(t.matmul(BK[tg][:, 0:512], sONR[:, :], sMLO[0:1, :],
                                 start=False, stop=True))
            t.wait_ge(sFin.h, 4)
            t.wait_ge(sLk.h, lkm["consts"])
            if pmax >= 0:
                t.wait_ge(sWs.h, 1)
            for j in range(NK):
                t.wait_ge(sXh.h, j + 1)
                for tg in range(4):
                    mm = t.matmul(pm[tg][0:48, 0:512], sENC[:, j * 48:(j + 1) * 48],
                                  sXH[:, j * T + tg * 512:j * T + tg * 512 + 512],
                                  start=(j == 0), stop=(j == 31))
                    if j == 31:
                        sTer.inc(mm)
                if pmax >= 0:
                    t.wait_ge(sMT.h, 16 * min(j + 4, ntile_max))
                    sAmm.inc(t.matmul(pA[0:48, 0:512], sWS[:, j * 48:(j + 1) * 48],
                                      sMTb[:, (j % MTS) * 512:(j % MTS) * 512 + 512],
                                      start=(j == 0), stop=(j == 31)))
            def a_mms(pp):
                t.wait_ge(sAh.h, pp)
                for o in range(32):
                    gidx = 32 * pp + o
                    t.wait_ge(sMT.h, 16 * min(gidx + 4, ntile_max))
                    mm = t.matmul(pA[0:48, 0:512], sWS[:, o * 48:(o + 1) * 48],
                                  sMTb[:, (gidx % MTS) * 512:(gidx % MTS) * 512 + 512],
                                  start=(o == 0), stop=(o == 31))
                    sAmm.inc(mm)
            if nch >= 1:
                t.wait_ge(sTr.h, 1)
                sBt.inc(t.transpose(paux[0:8, 0:128], sBM2[:, :], sIDN[:, :]))
                t.wait_ge(sBit.h, 1)
                sBx.inc(t.matmul(pbm[:, 0:CW], sBIT[0:8, 0:128], sR8[0:8, 0:CW],
                                 start=True, stop=True))
            if noai:
                for pp in range(1, pmax + 1):
                    a_mms(pp)
            for c in range(nch):
                p2 = c // 2 + 1
                for ts in range(NTS):
                    Q = 16 * c + ts
                    if Q >= 5:
                        t.wait_ge(sEpi.h, Q - 4)
                    if ts == 0:
                        t.wait_ge(sAh.h, c // 2 + 1)
                        t.wait_ge(sBd.h, 32 * (c // 2 + 1))
                        if c == 0:
                            t.wait_ge(sE1.h, 4)
                            t.wait_ge(sE40.h, 128)
                    for k in range(NK):
                        g = c * NK + k
                        if ts == 0:
                            t.wait_ge(sWcR.h, g + 1)
                        mm = t.matmul(PSB[Q % 5][:, 0:CW],
                                      sXH[:, k * T + ts * 128:k * T + ts * 128 + 128],
                                      sWC[:, (g % RING) * CW:(g % RING) * CW + CW],
                                      start=(k == 0), stop=False)
                        if ts == NTS - 1:
                            sWcF.inc(mm)
                    hb = ((c // 2) % 2) * 512 + (c % 2) * CW
                    sPs.inc(t.matmul(PSB[Q % 5][:, 0:CW],
                                     sE01[0:48, ts * 128:(ts + 1) * 128],
                                     sAH[0:48, hb:hb + CW],
                                     start=False, stop=True))
                    if ts == 4 and c + 1 < nch:
                        t.wait_ge(sTr.h, c + 2)
                        sBt.inc(t.transpose(paux[0:8, 0:128], sBM2[:, :], sIDN[:, :]))
                    if ts == 5 and c + 1 < nch:
                        t.wait_ge(sBit.h, c + 2)
                        t.wait_ge(sBm.h, c + 1)
                        sBx.inc(t.matmul(pbm[:, 0:CW], sBIT[0:8, 0:128],
                                         sR8[0:8, 0:CW], start=True, stop=True))
                    if p2 <= pmax and not noai:
                        o = (c % 2) * 16 + ts
                        gidx = 32 * p2 + o
                        t.wait_ge(sMT.h, 16 * min(gidx + 4, ntile_max))
                        if o == 0:
                            t.wait_ge(sAh.h, p2)
                        mm = t.matmul(pA[0:48, 0:512], sWS[:, o * 48:(o + 1) * 48],
                                      sMTb[:, (gidx % MTS) * 512:(gidx % MTS) * 512 + 512],
                                      start=(o == 0), stop=(o == 31))
                        sAmm.inc(mm)

        if nch != NCH:
            _CACHE[key] = nc
            return nc
        assert sLx.n == 16 * 512 and sW.n == 16 * NCH * NK
        assert sMT.n == 16 * 256 and sXre.n == 16 * NQ and sSt.n == 16 * NQ
        assert all(sm.n == 16 * 8 * NCH for sm in sMk)
        assert sXc.n == 128 and sSq.n == 128 and sTsq.n == 128
        assert sVr.n == 4 and sSd2.n == 4 and sTiv.n == 16 and sIvc.n == 16
        assert sMh.n == 4 and sMb.n == 4 and sFin.n == 4 and sE40.n == 16 * 8
        assert sXh.n == NK and sTer.n == 4 and sE1.n == 4 and sWs.n == 1 and sWp.n == 9
        assert sWcR.n == NCH * NK and sWcF.n == NCH * NK and sPs.n == NQ and sEpi.n == NQ
        assert sAmm.n == 256 and sAh.n == 8 and sBd.n == 32 * 8
        assert sTr.n == NCH and sBt.n == NCH and sBit.n == NCH and sBx.n == NCH and sBm.n == NCH
        assert sRd.n == NCH

        _CACHE[key] = nc
    return nc


def kernel(L0_attn_data, L0_attn_mask, L0_attn_weight, L0_attn_bias,
           L0_ffn_W_base, L0_ffn_b_base, L0_ffn_enc0, L0_ffn_enc1,
           L0_ffn_coefs0, L0_ffn_coefs1, L0_ffn_mask0, L0_ffn_mask1):
    asc = np.ascontiguousarray
    x = asc(np.asarray(L0_attn_data, dtype=np.float32)).reshape(NCORES * T, D)

    _wt = _perm_rows(asc(np.asarray(L0_attn_weight, np.float32)).T).reshape(D, NCH, CW)
    _wb = _perm_rows(asc(np.asarray(L0_ffn_W_base, np.float32)).T).reshape(D, NCH, CW)
    wwbT = asc(np.concatenate([_wt, _wb], axis=2).reshape(D, 2 * D))
    m32 = asc(np.asarray(L0_attn_mask, np.int32).T)
    mkT = _perm_rows(asc(m32.view(np.uint8)[:, 0::4]))
    MT = _build_MT()
    encT = np.zeros((D, 48), dtype=np.float32)
    encT[:, 0:32] = np.asarray(L0_ffn_enc0, np.float32).T
    encT[:, 32:40] = np.asarray(L0_ffn_enc1, np.float32).T
    encT = _perm_rows(encT)
    coef0 = asc(np.asarray(L0_ffn_coefs0, np.float32))
    coef1 = asc(np.asarray(L0_ffn_coefs1, np.float32))
    msk0 = asc(np.asarray(L0_ffn_mask0)).view(np.uint8)
    msk1 = asc(np.asarray(L0_ffn_mask1)).view(np.uint8)
    biasA = asc(np.asarray(L0_attn_bias, np.float32)).reshape(1, D)
    biasB = asc(np.asarray(L0_ffn_b_base, np.float32)).reshape(1, D)
    idn = np.eye(128, dtype=np.float32)
    R8 = np.kron(np.eye(8, dtype=np.float32), np.ones((1, 32), dtype=np.float32))
    R8 = asc(R8)

    shared = dict(wwbT=wwbT, mkT=mkT, MT=MT, encT=encT,
                  coef0=coef0, msk0=msk0, coef1=coef1, msk1=msk1,
                  biasA=biasA, biasB=biasB, idn=idn, R8=R8)
    in_maps = []
    for ci in range(NCORES):
        xs = x[ci * T:(ci + 1) * T]
        m = dict(shared)
        m["xT"] = _perm_rows(asc(xs.T))
        m["xtm"] = asc(xs)
        in_maps.append(m)

    nc = _build_nc()
    res = run_bass_kernel_spmd(nc, in_maps, core_ids=list(range(NCORES)))
    out = np.stack([np.asarray(res.results[i]["out"], dtype=np.float32)
                    for i in range(NCORES)], axis=0)
    return out.reshape(NCORES, T, D)
